# version 14
# baseline (speedup 1.0000x reference)
import numpy as np
import ml_dtypes
import concourse.bass as bass
import concourse.mybir as mybir
from concourse.bass_utils import run_bass_kernel_spmd

F32 = mybir.dt.float32
BF16 = mybir.dt.bfloat16
AF = mybir.ActivationFunctionType
ALU = mybir.AluOpType
AX = mybir.AxisListType

NCORES = 8
NB = 4
N = 2048
D = 1024
NCH = 8
NT = 16
NE = 16
TOPK = 2
EPS = 1e-5
SCL = 1.0 / 32.0

ENGS = ("tensor", "scalar", "vector", "gpsimd", "sync")


class Fw:

    def __init__(self):
        self.ops = []
        self.streams = {}

    def op(self, eng, fn, reads=(), writes=(), tag=""):
        self.ops.append(dict(eng=eng, fn=fn, reads=tuple(reads), writes=tuple(writes),
                             dma=None, tag=tag, inc=False, deps=None))

    def dma_stream(self, name, queue, nslots, sem_alloc):
        self.streams[name] = dict(queue=queue, sems=[sem_alloc() for _ in range(nslots)],
                                  uses=[0] * nslots, n=0, nslots=nslots)

    def dma(self, stream, fn, reads=(), writes=(), tag=""):
        st = self.streams[stream]
        slot = st["n"] % st["nslots"]
        prior = st["uses"][slot]
        st["uses"][slot] = prior + 1
        st["n"] += 1
        self.ops.append(dict(eng=st["queue"], fn=fn, reads=tuple(reads), writes=tuple(writes),
                             dma=(stream, slot, prior), tag=tag, inc=False, deps=None))

    def analyze(self):
        last_write = {}
        readers = {}
        for i, o in enumerate(self.ops):
            deps = set()
            for b in o["reads"]:
                w = last_write.get(b)
                if w is not None:
                    deps.add(w)
            for b in o["writes"]:
                w = last_write.get(b)
                if w is not None:
                    deps.add(w)
                for r in readers.get(b, ()):
                    deps.add(r)
            deps.discard(i)
            fd = []
            for j in deps:
                oj = self.ops[j]
                if (oj["dma"] is None and o["dma"] is None
                        and oj["eng"] == "tensor" and o["eng"] == "tensor"):
                    continue
                fd.append(j)
            o["deps"] = sorted(fd)
            for j in o["deps"]:
                if self.ops[j]["dma"] is None:
                    self.ops[j]["inc"] = True
            for b in o["reads"]:
                readers.setdefault(b, []).append(i)
            for b in o["writes"]:
                last_write[b] = i
                readers[b] = [i]
        cnt = {e: 0 for e in ENGS}
        for o in self.ops:
            if o["dma"] is None:
                if o["inc"]:
                    cnt[o["eng"]] += 1
                o["cnt"] = cnt[o["eng"]]

    def emit(self, block, eng_sems):
        self.analyze()
        guar = {e: {} for e in ENGS}

        def signal(j):
            oj = self.ops[j]
            if oj["dma"] is None:
                return ("E", oj["eng"]), oj["cnt"]
            stream, slot, prior = oj["dma"]
            return ("D", stream, slot), 16 * (prior + 1)

        def sem_handle(key):
            if key[0] == "E":
                return eng_sems[key[1]]
            return self.streams[key[1]]["sems"][key[2]]

        def run_engine(eng_name, eng):
            g = guar[eng_name]
            for o in self.ops:
                if o["eng"] != eng_name:
                    continue
                if o.get("final"):
                    for stname, st in sorted(self.streams.items()):
                        for slot in range(st["nslots"]):
                            if st["uses"][slot] == 0:
                                continue
                            k = ("D", stname, slot)
                            vv = 16 * st["uses"][slot]
                            if g.get(k, 0) < vv:
                                eng.wait_ge(st["sems"][slot], vv)
                                g[k] = vv
                waits = {}
                for j in o["deps"]:
                    k, vv = signal(j)
                    if waits.get(k, 0) < vv:
                        waits[k] = vv
                if o["dma"] is not None:
                    stream, slot, prior = o["dma"]
                    if prior > 0:
                        k = ("D", stream, slot)
                        if waits.get(k, 0) < 16 * prior:
                            waits[k] = 16 * prior
                for k, vv in sorted(waits.items(), key=lambda kv: str(kv[0])):
                    if g.get(k, 0) >= vv:
                        continue
                    eng.wait_ge(sem_handle(k), vv)
                    g[k] = vv
                instr = o["fn"]()
                if o["dma"] is not None:
                    stream, slot, prior = o["dma"]
                    instr.then_inc(self.streams[stream]["sems"][slot], 16)
                elif o["inc"]:
                    instr.then_inc(eng_sems[o["eng"]], 1)

        for name in ENGS:
            dec = getattr(block, name)

            def make(nm):
                def body(e):
                    run_engine(nm, e)
                return body
            dec(make(name))


def build_nc():
    nc = bass.Bass("TRN2", target_bir_lowering=False)
    t, sc, v, gp, sy = nc.tensor, nc.scalar, nc.vector, nc.gpsimd, nc.sync

    d_x_tok = nc.dram_tensor("x_tok", [NB * N, D], F32, kind="ExternalInput").ap()
    d_x_dm = nc.dram_tensor("x_dm", [NB * D, N], F32, kind="ExternalInput").ap()
    d_WrT = nc.dram_tensor("WrT", [NE * D, D], F32, kind="ExternalInput").ap()
    d_WsT = nc.dram_tensor("WsT", [2 * D, D], F32, kind="ExternalInput").ap()
    d_WspinT = nc.dram_tensor("WspinT", [D, D], F32, kind="ExternalInput").ap()
    d_WJT = nc.dram_tensor("WJT", [D, D], F32, kind="ExternalInput").ap()
    d_WglobT = nc.dram_tensor("WglobT", [D, D], F32, kind="ExternalInput").ap()
    d_centT = nc.dram_tensor("centT", [D, NE], F32, kind="ExternalInput").ap()
    d_br = nc.dram_tensor("br", [NE, D], F32, kind="ExternalInput").ap()
    d_bspinB = nc.dram_tensor("bspinB", [128, D], F32, kind="ExternalInput").ap()
    d_bsB2 = nc.dram_tensor("bsB2", [2 * 128, D], F32, kind="ExternalInput").ap()
    d_ebB = nc.dram_tensor("ebB", [128, NE], F32, kind="ExternalInput").ap()
    d_betacol = nc.dram_tensor("betacol", [128, 1], F32, kind="ExternalInput").ap()
    d_ident = nc.dram_tensor("identbf", [128, 128], BF16, kind="ExternalInput").ap()
    d_ones_col = nc.dram_tensor("ones_col", [128, 1], BF16, kind="ExternalInput").ap()
    d_ones_row = nc.dram_tensor("ones_row", [1, 128], BF16, kind="ExternalInput").ap()
    d_ones_rowf = nc.dram_tensor("ones_rowf", [1, 128], F32, kind="ExternalInput").ap()

    d_out = nc.dram_tensor("out", [NB * N, D], F32, kind="ExternalOutput").ap()
    d_gsp = nc.dram_tensor("g_spill", [NB * N, D], F32, kind="Internal").ap()
    d_wc = nc.dram_tensor("w_cache", [6 * 128, 4 * 1024], BF16, kind="Internal").ap()
    d_ssp = nc.dram_tensor("s_spill", [NB * N, D], F32, kind="Internal").ap()

    import contextlib
    ctx = contextlib.ExitStack()

    def sb(name, shape, dt):
        return ctx.enter_context(nc.sbuf_tensor("sb_" + name, shape, dt))

    def psa(name, shape, dt):
        return ctx.enter_context(nc.psum_tensor(name, shape, dt))

    A = sb("A", [128, 16 * 1024], BF16)
    Bt = sb("Bt", [128, 16 * 1024], BF16)
    Ct = sb("Ct", [128, 16 * 1024], BF16)
    Dt = sb("Dt", [128, 24 * 512], F32)
    W = [sb(f"W{i}", [128, 4 * 1024], BF16) for i in range(4)]
    J = sb("J", [128, 8 * 512], BF16)

    bspinB = sb("bspinB", [128, D], F32)
    bsB = sb("bsB", [128, D], F32)
    ebB = sb("ebB", [128, NE], F32)
    betacol = sb("betacol", [128, 1], F32)
    identbf = sb("identbf", [128, 128], BF16)
    ones_col = sb("ones_col", [128, 1], BF16)
    ones_row = sb("ones_row", [1, 128], BF16)
    ones_rowf = sb("ones_rowf", [1, 128], F32)
    centT = sb("centT", [128, NCH * NE], BF16)
    br_bf = sb("br_bf", [16, D], BF16)
    gT_bf = sb("gT_bf", [16, N], BF16)
    m_sum = sb("m_sum", [128, NT], F32)
    q_sum = sb("q_sum", [128, NT], F32)
    m_col = sb("m_col", [128, NT], F32)
    r_col = sb("r_col", [128, NT], F32)
    rm_col = sb("rm_col", [128, NT], F32)
    halfr = sb("halfr", [128, NT], F32)
    st1 = sb("st1", [128, NT], F32)
    st2 = sb("st2", [128, NT], F32)
    g_r = sb("g_r", [128, NT * NE], F32)
    gr_r = sb("gr_r", [128, NT * NE], F32)
    g_bf = sb("g_bf", [128, NT * NE], BF16)
    top8 = sb("top8", [128, 8], F32)
    pmneg = sb("pmneg", [128, 1], F32)
    precip = sb("precip", [128, 1], F32)
    snorm2 = sb("snorm2", [128, NT], F32)
    smax2c = sb("smax2c", [128, 1], F32)
    smax2bf = sb("smax2bf", [128, 1], BF16)
    cval = sb("cval", [1, 2], F32)
    cneg = sb("cneg", [128, 1], F32)
    recipd = sb("recipd", [128, 4], F32)
    mf_bf = sb("mf_bf", [128, NCH], BF16)
    mfs = sb("mfs", [128, 1], F32)

    PB = [psa(f"PB{i}", [128, 512], F32) for i in range(6)]
    PBb = psa("PBb", [128, 1024], BF16)
    PB7 = psa("PB7", [128, 512], F32)
    psb = {0: "PB0", 1: "PB1", 2: "PB2", 3: "PB3", 4: "PB4", 5: "PB5", 7: "PB7"}
    P7ALL = [f"P7_{u}" for u in range(32)]

    fw = Fw()
    sems = {e: ctx.enter_context(nc.semaphore(f"es_{e}")) for e in ENGS}

    _semn = [0]

    def mksem():
        _semn[0] += 1
        return ctx.enter_context(nc.semaphore(f"dsem{_semn[0]}"))

    fw.dma_stream("xcl", "gpsimd", 2, mksem)
    fw.dma_stream("wl", "gpsimd", 4, mksem)
    fw.dma_stream("cl", "gpsimd", 2, mksem)
    fw.dma_stream("xt", "sync", 3, mksem)
    fw.dma_stream("sw", "scalar", 4, mksem)
    fw.dma_stream("sr", "sync", 4, mksem)
    fw.dma_stream("gout", "scalar", 4, mksem)
    fw.dma_stream("cst", "sync", 2, mksem)
    fw.dma_stream("wc", "scalar", 2, mksem)
    fw.dma_stream("wls", "sync", 2, mksem)
    fw.dma_stream("wlc", "scalar", 2, mksem)

    def aid(lo, hi):
        return [f"A{u}" for u in range(lo, hi)]

    def did(u):
        return f"D{u}"

    def wid(slot):
        return f"W{slot}"

    def xcT(c, col0, w):
        return A[:, c * 2048 + col0: c * 2048 + col0 + w]

    def xcT_bufs(c, col0, w):
        lo = (c * 2048 + col0) // 1024
        hi = (c * 2048 + col0 + w + 1023) // 1024
        return aid(lo, hi)

    def AT_ap(pp, kt, q0, w):
        base = kt * 512 + q0
        return A[:, base:base + w]

    def AT_bufs(pp, kt):
        u = (kt * 512) // 1024
        return aid(u, u + 1)

    def sbf_ap(j, col0, w):
        return Bt[:, j * 1024 + col0: j * 1024 + col0 + w]

    def sbf_bufs(j):
        return [f"B{j}"]

    def sT_ap(c, col0, w):
        return Ct[:, c * 2048 + col0: c * 2048 + col0 + w]

    def sT_bufs(c):
        return [f"C{2 * c}", f"C{2 * c + 1}"]

    def routed_ap(j, col0, w):
        tens = Bt if j < 8 else Ct
        jj = j if j < 8 else j - 8
        return tens[:, jj * 2048: jj * 2048 + 2048].bitcast(F32)[:, col0:col0 + w]

    def routed_bufs(j):
        if j < 8:
            return [f"B{2 * j}", f"B{2 * j + 1}"]
        return [f"C{2 * (j - 8)}", f"C{2 * (j - 8) + 1}"]

    def eT_ap(c, w=512):
        return J[:, c * 512: c * 512 + w]

    def eT_bufs(c):
        return [f"J{c}"]

    def dpair_x(i):
        return Dt[:, (2 * i) * 512:(2 * i + 2) * 512]

    def xr_bufs(i):
        return [did(2 * i), did(2 * i + 1)]

    def dpair_g(i):
        return Dt[:, (4 + 2 * i) * 512:(6 + 2 * i) * 512]

    def gr_bufs(i):
        return [did(4 + 2 * i), did(5 + 2 * i)]

    def dpair_s(i):
        return Dt[:, (10 + 2 * i) * 512:(12 + 2 * i) * 512]

    def sr_bufs(i):
        return [did(10 + 2 * i), did(11 + 2 * i)]

    def dpair_o(i):
        return Dt[:, (16 + 2 * i) * 512:(18 + 2 * i) * 512]

    def or_bufs(i):
        return [did(16 + 2 * i), did(17 + 2 * i)]

    GB = Dt[:, 20 * 512:22 * 512]
    GB_bufs = [did(20), did(21)]
    MSC = Dt[:, 22 * 512:23 * 512]
    MSC_bufs = [did(22)]
    mrow_bf_ap = Dt[0:1, 23 * 512:23 * 512 + 256].bitcast(BF16)
    ROW_bufs = [did(23)]
    sm_row = Dt[0:1, 23 * 512 + 256:23 * 512 + 384]
    grow = Dt[0:1, 20 * 512:20 * 512 + 1024]
    GROW_bufs = [did(20), did(21)]

    fw.dma("cst", lambda: sy.dma_start(bspinB[:], d_bspinB), writes=["bspinB"])
    fw.dma("cst", lambda: sy.dma_start(ebB[:], d_ebB), writes=["ebB"])
    fw.dma("cst", lambda: sy.dma_start(betacol[:], d_betacol), writes=["betacol"])
    fw.dma("cst", lambda: sy.dma_start(identbf[:], d_ident), writes=["identbf"])
    fw.dma("cst", lambda: sy.dma_start(ones_col[:], d_ones_col), writes=["ones_col"])
    fw.dma("cst", lambda: sy.dma_start(ones_row[:], d_ones_row), writes=["ones_row"])
    fw.dma("cst", lambda: sy.dma_start(ones_rowf[:], d_ones_rowf), writes=["ones_rowf"])
    fw.dma("cst", lambda: sy.dma_start(dpair_x(0), d_bsB2[0:128, :]), writes=xr_bufs(0))
    fw.dma("cst", lambda: sy.dma_start(dpair_g(0), d_bsB2[128:256, :]), writes=gr_bufs(0))
    fw.op("vector", lambda: v.tensor_add(bsB[:], dpair_x(0), dpair_g(0)),
          reads=xr_bufs(0) + gr_bufs(0), writes=["bsB"])
    fw.op("vector", lambda: v.tensor_scalar_mul(bsB[:], bsB[:], 0.5),
          reads=["bsB"], writes=["bsB"])
    fw.dma("cl", lambda: gp.dma_start(
        centT[:].rearrange("p (c e) -> p c e", e=NE),
        d_centT.rearrange("(c p) e -> p c e", p=128)), writes=["centT"])
    fw.dma("cl", lambda: gp.dma_start(br_bf[:], d_br), writes=["br_bf"])

    def load_w_half(slot, dram, row0, tag):
        def fn(sl=slot, dr=dram, r0=row0):
            return gp.dma_start(
                W[sl][:].rearrange("p (c o) -> p c o", o=D),
                dr[r0:r0 + 512, :].rearrange("(c p) o -> p c o", p=128))
        fw.dma("wl", fn, writes=[wid(slot)], tag=tag)

    def load_w_cached(slot, dram, row0, cidx, first, stream, tag):
        if first:
            load_w_half(slot, dram, row0, tag)
            eng = sc if fw.streams["wc"]["queue"] == "scalar" else sy
            fw.dma("wc", (lambda sl=slot, ci=cidx, eng=eng: eng.dma_start(
                d_wc[ci * 128:(ci + 1) * 128, :], W[sl][:])),
                reads=[wid(slot)], writes=[f"wc{cidx}"], tag=tag + "st")
        else:
            qn = fw.streams[stream]["queue"]
            eng = {"scalar": sc, "sync": sy, "gpsimd": gp}[qn]
            fw.dma(stream, (lambda sl=slot, ci=cidx, eng=eng: eng.dma_start(
                W[sl][:], d_wc[ci * 128:(ci + 1) * 128, :])),
                reads=[f"wc{cidx}"], writes=[wid(slot)], tag=tag)

    def w_rhs(pair, c, oh):
        sl = pair[c // 4]
        base = (c % 4) * 1024 + oh * 512
        return W[sl][:, base:base + 512]

    def w_sl(pair, c, col0, w):
        sl = pair[c // 4]
        base = (c % 4) * 1024 + col0
        return W[sl][:, base:base + w]

    wl_pairs = [(0, 1), (2, 3)]

    def emit_xcl_half(b, half):
        fw.dma("xcl", (lambda b=b, half=half: gp.dma_start(
            A[:, half * 8192:(half + 1) * 8192].rearrange("p (c n) -> p c n", n=N),
            d_x_dm[b * D + half * 512: b * D + (half + 1) * 512, :]
            .rearrange("(c p) n -> p c n", p=128))),
            writes=aid(8 * half, 8 * half + 8), tag=f"b{b}_xcl{half}")

    def emit_batch(b):
        tA = f"b{b}_"

        emit_xcl_half(b, 0)
        emit_xcl_half(b, 1)

        for c in range(NCH):
            fw.op("scalar", (lambda c=c: sc.activation(
                J[:, 0:2048], xcT(c, 0, 2048), AF.Square)),
                reads=xcT_bufs(c, 0, 2048), writes=["J0", "J1", "J2", "J3"],
                tag=tA + "sq")
            for j in range(NT):
                fw.op("tensor", (lambda c=c, j=j: t.matmul(
                    PB[2][:, j:j + 1], xcT(c, j * 128, 128), ones_col[:],
                    start=True, stop=True)),
                    reads=xcT_bufs(c, j * 128, 128) + ["ones_col"], writes=["PB2"],
                    tag=tA + "mc")
                fw.op("tensor", (lambda c=c, j=j: t.matmul(
                    PB[3][:, j:j + 1], J[:, j * 128:(j + 1) * 128], ones_col[:],
                    start=True, stop=True)),
                    reads=["J0", "J1", "J2", "J3", "ones_col"], writes=["PB3"],
                    tag=tA + "qc")
            if c == 0:
                fw.op("vector", (lambda: v.tensor_copy(m_sum[:], PB[2][:, 0:NT])),
                      reads=["PB2"], writes=["m_sum"], tag=tA + "mca")
                fw.op("vector", (lambda: v.tensor_copy(q_sum[:], PB[3][:, 0:NT])),
                      reads=["PB3"], writes=["q_sum"], tag=tA + "qca")
            else:
                fw.op("vector", (lambda: v.tensor_add(m_sum[:], m_sum[:], PB[2][:, 0:NT])),
                      reads=["m_sum", "PB2"], writes=["m_sum"], tag=tA + "mca")
                fw.op("vector", (lambda: v.tensor_add(q_sum[:], q_sum[:], PB[3][:, 0:NT])),
                      reads=["q_sum", "PB3"], writes=["q_sum"], tag=tA + "qca")

        for quad in range(4):
            for jj in range(4):
                j = quad * 4 + jj
                for c in range(NCH):
                    fw.op("tensor", (lambda c=c, j=j, jj=jj: t.matmul(
                        PB[3][0:1, jj * 128:jj * 128 + 128], ones_col[:],
                        xcT(c, j * 128, 128), start=(c == 0), stop=(c == NCH - 1))),
                        reads=["ones_col"] + xcT_bufs(c, j * 128, 128), writes=["PB3"],
                        tag=tA + "mrow")
            fw.op("scalar", (lambda: sc.activation(mrow_bf_ap, PB[3][0:1, :], AF.Copy,
                                                   scale=1.0 / D)),
                  reads=["PB3"], writes=ROW_bufs, tag=tA + "mrowe")
            fw.op("tensor", (lambda: t.matmul(PB[2][:], ones_row[:], mrow_bf_ap,
                                              start=True, stop=True)),
                  reads=["ones_row"] + ROW_bufs, writes=["PB2"], tag=tA + "mB")
            for c in range(NCH):
                fw.op("vector", (lambda c=c, quad=quad: v.tensor_sub(
                    xcT(c, quad * 512, 512), xcT(c, quad * 512, 512), PB[2][:])),
                    reads=xcT_bufs(c, quad * 512, 512) + ["PB2"],
                    writes=xcT_bufs(c, quad * 512, 512), tag=tA + "ctr")

        fw.op("vector", lambda: v.tensor_scalar_mul(m_col[:], m_sum[:], 1.0 / D),
              reads=["m_sum"], writes=["m_col"])
        fw.op("vector", lambda: v.tensor_scalar_mul(st1[:], q_sum[:], 1.0 / D),
              reads=["q_sum"], writes=["st1"])
        fw.op("vector", lambda: v.tensor_mul(st2[:], m_col[:], m_col[:]),
              reads=["m_col"], writes=["st2"])
        fw.op("vector", lambda: v.tensor_sub(st1[:], st1[:], st2[:]),
              reads=["st1", "st2"], writes=["st1"])
        fw.op("vector", lambda: v.tensor_scalar_add(st1[:], st1[:], EPS),
              reads=["st1"], writes=["st1"])
        fw.op("scalar", lambda: sc.activation(st2[:], st1[:], AF.Sqrt),
              reads=["st1"], writes=["st2"])
        fw.op("vector", lambda: v.reciprocal(r_col[:], st2[:]),
              reads=["st2"], writes=["r_col"])
        fw.op("vector", lambda: v.tensor_mul(st2[:], r_col[:], r_col[:]),
              reads=["r_col"], writes=["st2"])
        fw.op("vector", lambda: v.tensor_mul(st2[:], st2[:], st1[:]),
              reads=["st2", "st1"], writes=["st2"])
        fw.op("vector", lambda: v.tensor_scalar(st2[:], st2[:], -0.5, 1.5, ALU.mult, ALU.add),
              reads=["st2"], writes=["st2"])
        fw.op("vector", lambda: v.tensor_mul(r_col[:], r_col[:], st2[:]),
              reads=["r_col", "st2"], writes=["r_col"])
        fw.op("vector", lambda: v.tensor_mul(rm_col[:], r_col[:], m_col[:]),
              reads=["r_col", "m_col"], writes=["rm_col"])
        fw.op("vector", lambda: v.tensor_scalar_mul(halfr[:], r_col[:], 0.5),
              reads=["r_col"], writes=["halfr"])

        for j in range(NT):
            for c in range(NCH):
                fw.op("tensor", (lambda c=c, j=j: t.matmul(
                    PB7[:, j * NE:(j + 1) * NE], xcT(c, j * 128, 128), centT[:, c * NE:(c + 1) * NE],
                    start=(c == 0), stop=(c == NCH - 1))),
                    reads=xcT_bufs(c, j * 128, 128) + ["centT"], writes=[f"P7_{j}"],
                    tag=tA + "rt")
        for j in range(NT):
            fw.op("vector", (lambda j=j: v.tensor_scalar(
                g_r[:, j * NE:(j + 1) * NE], PB7[:, j * NE:(j + 1) * NE], r_col[:, j:j + 1], None, ALU.mult)),
                reads=[f"P7_{j}", "r_col"], writes=["g_r"], tag=tA + "rte")
            fw.op("vector", (lambda j=j: v.tensor_add(
                g_r[:, j * NE:(j + 1) * NE], g_r[:, j * NE:(j + 1) * NE], ebB[:])),
                reads=["g_r", "ebB"], writes=["g_r"])
            fw.op("vector", (lambda j=j: v.reduce_max(
                pmneg[:], g_r[:, j * NE:(j + 1) * NE], axis=AX.X)),
                reads=["g_r"], writes=["pmneg"])
            fw.op("vector", lambda: v.tensor_scalar_mul(pmneg[:], pmneg[:], -1.0),
                  reads=["pmneg"], writes=["pmneg"])
            fw.op("scalar", (lambda j=j: sc.activation(
                g_r[:, j * NE:(j + 1) * NE], g_r[:, j * NE:(j + 1) * NE], AF.Exp,
                bias=pmneg[:, 0:1], accum_out=precip[:, 0:1])),
                reads=["g_r", "pmneg"], writes=["g_r", "precip"], tag=tA + "rtx")
            fw.op("vector", lambda: v.reciprocal(precip[:], precip[:]),
                  reads=["precip"], writes=["precip"])
            fw.op("vector", (lambda j=j: v.tensor_scalar(
                g_r[:, j * NE:(j + 1) * NE], g_r[:, j * NE:(j + 1) * NE],
                precip[:, 0:1], None, ALU.mult)),
                reads=["g_r", "precip"], writes=["g_r"])
            fw.op("vector", (lambda j=j: v.max(
                out=top8[:], in_=g_r[:, j * NE:(j + 1) * NE])),
                reads=["g_r"], writes=["top8"])
            fw.op("vector", (lambda j=j: v.tensor_scalar(
                st1[:, 0:NE], g_r[:, j * NE:(j + 1) * NE], top8[:, TOPK - 1:TOPK],
                None, ALU.is_ge)),
                reads=["g_r", "top8"], writes=["st1"])
            fw.op("vector", (lambda j=j: v.tensor_mul(
                g_r[:, j * NE:(j + 1) * NE], g_r[:, j * NE:(j + 1) * NE], st1[:, 0:NE])),
                reads=["g_r", "st1"], writes=["g_r"])
            fw.op("vector", (lambda j=j: v.tensor_scalar(
                gr_r[:, j * NE:(j + 1) * NE], g_r[:, j * NE:(j + 1) * NE],
                r_col[:, j:j + 1], None, ALU.mult)),
                reads=["g_r", "r_col"], writes=["gr_r"])
            fw.op("vector", (lambda j=j: v.tensor_copy(
                g_bf[:, j * NE:(j + 1) * NE], g_r[:, j * NE:(j + 1) * NE])),
                reads=["g_r"], writes=[f"gbf{j}"])

        def emit_gT_br(j):
            fw.op("tensor", (lambda j=j: t.transpose(
                PBb[0:NE, 0:128], g_bf[:, j * NE:(j + 1) * NE], identbf[:])),
                reads=[f"gbf{j}", "identbf"], writes=["PBb"], tag=tA + "gT")
            fw.op("vector", (lambda j=j: v.tensor_copy(
                gT_bf[0:16, j * 128:(j + 1) * 128], PBb[0:NE, 0:128])),
                reads=["PBb"], writes=[f"gTb{j}"])
            for oh in range(2):
                pbk = 4 + (j * 2 + oh) % 2
                pbap = PB[4] if pbk == 4 else PB[5]
                fw.op("tensor", (lambda j=j, oh=oh, pbap=pbap: t.matmul(
                    pbap[:], gT_bf[0:16, j * 128:(j + 1) * 128],
                    br_bf[0:16, oh * 512:(oh + 1) * 512], start=True, stop=True)),
                    reads=[f"gTb{j}", "br_bf"], writes=[psb[pbk]], tag=tA + "br")
                fw.op("vector", (lambda j=j, oh=oh, pbap=pbap, pbk=pbk: v.scalar_tensor_tensor(
                    routed_ap(j, oh * 512, 512), pbap[:], rm_col[:, j:j + 1],
                    bsB[:, oh * 512:(oh + 1) * 512], ALU.subtract, ALU.add)),
                    reads=[psb[pbk], "rm_col", "bsB"], writes=routed_bufs(j),
                    tag=tA + "brv")

        for e in range(NE + 2):
            pair = wl_pairs[e % 2]
            dram = d_WrT if e < NE else d_WsT
            row = e * D if e < NE else (e - NE) * D
            load_w_half(pair[0], dram, row, tA + f"w{e}a")
            load_w_half(pair[1], dram, row + 512, tA + f"w{e}b")
            for j in range(NT):
                pb0 = 2 * (j % 2)
                for c in range(NCH):
                    for oh in range(2):
                        fw.op("tensor", (lambda c=c, j=j, oh=oh, pb=pb0 + oh, pair=pair: t.matmul(
                            PB[pb][:], xcT(c, j * 128, 128), w_rhs(pair, c, oh),
                            start=(c == 0), stop=(c == NCH - 1))),
                            reads=xcT_bufs(c, j * 128, 128) + [wid(pair[c // 4])],
                            writes=[psb[pb0 + oh]], tag=tA + f"e{e}")
                if e == 0:
                    emit_gT_br(j)
                for oh in range(2):
                    pb = pb0 + oh
                    if e < NE:
                        scal = gr_r[:, j * NE + e:j * NE + e + 1]
                        srd = "gr_r"
                    else:
                        scal = halfr[:, j:j + 1]
                        srd = "halfr"
                    fw.op("vector", (lambda j=j, oh=oh, pb=pb, scal=scal: v.scalar_tensor_tensor(
                        routed_ap(j, oh * 512, 512), PB[pb][:], scal,
                        routed_ap(j, oh * 512, 512), ALU.mult, ALU.add)),
                        reads=[psb[pb], srd] + routed_bufs(j),
                        writes=routed_bufs(j), tag=tA + f"ev{e}")

        first = (b == 0)
        load_w_cached(0, d_WspinT, 0, 0, first, "wls", tA + "spA")
        load_w_cached(1, d_WspinT, 512, 1, first, "wls", tA + "spB")
        load_w_cached(2, d_WJT, 0, 2, first, "wlc", tA + "wjA")
        load_w_cached(3, d_WJT, 512, 3, first, "wlc", tA + "wjB")

        for j in range(NT):
            xr = j % 2
            gi = j % 3
            fw.dma("xt", (lambda b=b, j=j, xr=xr: sy.dma_start(
                dpair_x(xr), d_x_tok[b * N + j * 128: b * N + j * 128 + 128, :])),
                writes=xr_bufs(xr), tag=tA + "xr1")
            fw.op("vector", (lambda j=j, xr=xr, gi=gi: v.scalar_tensor_tensor(
                dpair_g(gi), dpair_x(xr), r_col[:, j:j + 1], routed_ap(j, 0, 1024),
                ALU.mult, ALU.add)),
                reads=xr_bufs(xr) + ["r_col"] + routed_bufs(j), writes=gr_bufs(gi),
                tag=tA + "xn")
            fw.op("scalar", (lambda gi=gi: sc.activation(
                dpair_g(gi), dpair_g(gi), AF.Silu)),
                reads=gr_bufs(gi), writes=gr_bufs(gi), tag=tA + "silu")
            fw.dma("sw", (lambda b=b, j=j, gi=gi: sc.dma_start(
                d_gsp[b * N + j * 128: b * N + j * 128 + 128, :], dpair_g(gi))),
                reads=gr_bufs(gi), writes=[f"hg{b}_{j}"], tag=tA + "gsp")

        spair = (0, 1)
        jpair = (2, 3)
        def emit_sT(j):
            for c in range(NCH):
                fw.op("tensor", (lambda j=j, c=c: t.transpose(
                    PBb[:, c * 128:(c + 1) * 128], sbf_ap(j, c * 128, 128), identbf[:])),
                    reads=sbf_bufs(j) + ["identbf"], writes=["PBb"], tag=tA + "sT")
            for c in range(NCH):
                fw.op("vector", (lambda j=j, c=c: v.tensor_copy(
                    sT_ap(c, j * 128, 128), PBb[:, c * 128:(c + 1) * 128])),
                    reads=["PBb"], writes=sT_bufs(c), tag=tA + "sTe")

        for j in range(NT):
            ss = j % 3
            pb0 = 2 * (j % 2)
            for c in range(NCH):
                for oh in range(2):
                    fw.op("tensor", (lambda c=c, j=j, oh=oh, pb=pb0 + oh: t.matmul(
                        PB[pb][:], xcT(c, j * 128, 128), w_rhs(spair, c, oh),
                        start=(c == 0), stop=(c == NCH - 1))),
                        reads=xcT_bufs(c, j * 128, 128) + [wid(spair[c // 4])],
                        writes=[psb[pb0 + oh]], tag=tA + "sm")
            for oh in range(2):
                pb = pb0 + oh
                fw.op("vector", (lambda j=j, oh=oh, pb=pb, ss=ss: v.scalar_tensor_tensor(
                    dpair_s(ss)[:, oh * 512:(oh + 1) * 512], PB[pb][:],
                    r_col[:, j:j + 1], bspinB[:, oh * 512:(oh + 1) * 512],
                    ALU.mult, ALU.add)),
                    reads=[psb[pb], "r_col", "bspinB"], writes=[sr_bufs(ss)[oh]],
                    tag=tA + "se")
            fw.op("scalar", (lambda j=j, ss=ss: sc.activation(
                sbf_ap(j, 0, 1024), dpair_s(ss), AF.Copy)),
                reads=sr_bufs(ss), writes=sbf_bufs(j), tag=tA + "sc")
            fw.op("vector", (lambda j=j: v.scalar_tensor_tensor(
                J[:, 2048:3072], sbf_ap(j, 0, 1024), 1.0, sbf_ap(j, 0, 1024),
                ALU.mult, ALU.mult, accum_out=snorm2[:, j:j + 1])),
                reads=sbf_bufs(j), writes=["J4", "J5", "snorm2"],
                tag=tA + "sn")
            fw.dma("sw", (lambda b=b, j=j, ss=ss: sc.dma_start(
                d_ssp[b * N + j * 128: b * N + j * 128 + 128, :], dpair_s(ss))),
                reads=sr_bufs(ss), writes=[f"hs{b}_{j}"], tag=tA + "ssp")
            if j > 0:
                emit_sT(j - 1)
        emit_sT(NT - 1)

        fw.op("vector", lambda: v.reduce_max(smax2c[:], snorm2[:], axis=AX.X),
              reads=["snorm2"], writes=["smax2c"], tag=tA + "sm2")
        fw.op("vector", lambda: v.tensor_copy(smax2bf[:], smax2c[:]),
              reads=["smax2c"], writes=["smax2bf"])
        fw.op("tensor", lambda: t.transpose(PBb[0:1, 0:128], smax2bf[:], identbf[:]),
              reads=["smax2bf", "identbf"], writes=["PBb"], tag=tA + "smT")
        fw.op("vector", lambda: v.tensor_copy(sm_row, PBb[0:1, 0:128]),
              reads=["PBb"], writes=ROW_bufs)
        fw.op("vector", lambda: v.reduce_max(cval[:, 0:1], sm_row, axis=AX.X),
              reads=ROW_bufs, writes=["cval"])

        load_w_cached(0, d_WglobT, 0, 4, first, "wls", tA + "wgA")
        load_w_cached(1, d_WglobT, 512, 5, first, "wls", tA + "wgB")

        def emit_mfglob():
            gpair = (0, 1)
            for oh in range(2):
                for kt in range(NT):
                    fw.op("tensor", (lambda kt=kt, oh=oh: t.matmul(
                        PB[3][0:1, 0:512], ones_col[:], sbf_ap(kt, oh * 512, 512),
                        start=(kt == 0), stop=(kt == NT - 1))),
                        reads=["ones_col"] + sbf_bufs(kt), writes=["PB3"],
                        tag=tA + "mfm")
                fw.op("scalar", (lambda oh=oh: sc.activation(
                    mrow_bf_ap, PB[3][0:1, 0:512], AF.Copy, scale=1.0 / N)),
                    reads=["PB3"], writes=ROW_bufs, tag=tA + "mfe")
                for c4 in range(4):
                    c = oh * 4 + c4
                    fw.op("tensor", (lambda c=c, c4=c4: t.transpose(
                        PBb[:, c4 * 128:c4 * 128 + 1],
                        mrow_bf_ap[0:1, c4 * 128:(c4 + 1) * 128], identbf[0:1, 0:1])),
                        reads=ROW_bufs + ["identbf"], writes=["PBb"], tag=tA + "mfT")
                    fw.op("vector", (lambda c=c, c4=c4: v.tensor_copy(
                        mf_bf[:, c:c + 1], PBb[:, c4 * 128:c4 * 128 + 1])),
                        reads=["PBb"], writes=["mf_bf"], tag=tA + "mfTe")
            for oh in range(2):
                for c in range(NCH):
                    fw.op("tensor", (lambda c=c, oh=oh: t.matmul(
                        PB[3][0:1, 0:512], mf_bf[:, c:c + 1], w_rhs(gpair, c, oh),
                        start=(c == 0), stop=(c == NCH - 1))),
                        reads=["mf_bf", wid(gpair[c // 4])], writes=["PB3"], tag=tA + "gl")
                fw.op("scalar", (lambda oh=oh: sc.activation(
                    grow[0:1, oh * 512:(oh + 1) * 512], PB[3][0:1, 0:512], AF.Copy)),
                    reads=["PB3"], writes=GROW_bufs, tag=tA + "gle")
            for oh in range(2):
                fw.op("tensor", (lambda oh=oh: t.matmul(
                    PB[3][:], ones_rowf[:], grow[0:1, oh * 512:(oh + 1) * 512],
                    start=True, stop=True)),
                    reads=["ones_rowf"] + GROW_bufs, writes=["PB3"], tag=tA + "gb")
                fw.op("vector", (lambda oh=oh: v.tensor_copy(
                    GB[:, oh * 512:(oh + 1) * 512], PB[3][:])),
                    reads=["PB3"], writes=[GB_bufs[oh]], tag=tA + "gbe")

        for qb in range(4):
            pp = qb % 2
            q0 = qb * 512
            def e2m_mm(c):
                half = c % 2
                fw.op("tensor", (lambda c=c, half=half: t.matmul(
                    PB[5][0:1, 0:512], ones_col[:],
                    MSC[:].bitcast(BF16)[:, half * 512:(half + 1) * 512],
                    start=(c == 0), stop=(c == NCH - 1))),
                    reads=["ones_col", f"D22{'ab'[half]}"], writes=["PB5"],
                    tag=tA + "e2m")

            for c in range(NCH):
                pbk = 4 if (c % 2 == 0) else 7
                pbap = PB[4] if pbk == 4 else PB7
                for c2 in range(NCH):
                    fw.op("tensor", (lambda c=c, c2=c2, pbap=pbap, q0=q0: t.matmul(
                        pbap[:], w_sl(jpair, c2, c * 128, 128), sT_ap(c2, q0, 512),
                        start=(c2 == 0), stop=(c2 == NCH - 1))),
                        reads=[wid(jpair[c2 // 4])] + sT_bufs(c2),
                        writes=(P7ALL if pbk == 7 else [psb[pbk]]), tag=tA + f"en{qb}")
                fw.op("scalar", (lambda c=c, pbap=pbap: sc.activation(
                    eT_ap(c), pbap[:], AF.Copy)),
                    reads=(P7ALL if pbk == 7 else [psb[pbk]]), writes=eT_bufs(c), tag=tA + f"ene{qb}")
                fw.op("scalar", (lambda c=c, half=c % 2: sc.activation(
                    MSC[:].bitcast(BF16)[:, half * 512:(half + 1) * 512],
                    eT_ap(c), AF.Square)),
                    reads=eT_bufs(c), writes=[f"D22{'ab'[c % 2]}"], tag=tA + "e2")
                if c >= 2:
                    e2m_mm(c - 2)
            e2m_mm(NCH - 2)
            e2m_mm(NCH - 1)
            def qk_group(kt):
                pb = 2 + (kt % 2)
                for c in range(NCH):
                    fw.op("tensor", (lambda c=c, kt=kt, pb=pb, q0=q0: t.matmul(
                        PB[pb][:], sT_ap(c, kt * 128, 128), eT_ap(c),
                        start=(c == 0), stop=(c == NCH - 1))),
                        reads=sT_bufs(c) + eT_bufs(c), writes=[psb[pb]],
                        tag=tA + f"qk{qb}")

            def exp_group(kt):
                pb = 2 + (kt % 2)
                fw.op("scalar", (lambda kt=kt, pb=pb, pp=pp: sc.activation(
                    AT_ap(pp, kt, 0, 512), PB[pb][:], AF.Exp, scale=SCL,
                    bias=cneg[:, 0:1])),
                    reads=[psb[pb], "cneg"], writes=AT_bufs(pp, kt), tag=tA + f"xp{qb}")

            qk_group(0)
            qk_group(1)
            fw.op("vector", lambda: v.reduce_max(cval[:, 1:2], PB[5][0:1, 0:512], axis=AX.X),
                  reads=["PB5"], writes=["cval"], tag=tA + "emx")
            fw.op("vector", lambda: v.tensor_add(cval[:, 1:2], cval[:, 1:2], cval[:, 0:1]),
                  reads=["cval"], writes=["cval"])
            fw.op("vector", lambda: v.tensor_scalar(
                cval[:, 1:2], cval[:, 1:2], -1.02 / 64.0, None, ALU.mult),
                reads=["cval"], writes=["cval"])
            fw.op("tensor", lambda: t.matmul(
                PB[5][:, 0:1], ones_rowf[:], cval[0:1, 1:2], start=True, stop=True),
                reads=["ones_rowf", "cval"], writes=["PB5"], tag=tA + "cb")
            fw.op("vector", lambda: v.tensor_copy(cneg[:], PB[5][:, 0:1]),
                  reads=["PB5"], writes=["cneg"])
            exp_group(0)
            exp_group(1)
            for kt in range(2, NT):
                qk_group(kt)
                exp_group(kt)
            if qb == 0:
                emit_mfglob()

            for jt in range(4):
                jglob = qb * 4 + jt
                for kt in range(NT):
                    fw.op("tensor", (lambda kt=kt, jt=jt, pp=pp: t.matmul(
                        PB7[:, jt:jt + 1], AT_ap(pp, kt, jt * 128, 128), ones_col[:],
                        start=(kt == 0), stop=(kt == NT - 1))),
                        reads=AT_bufs(pp, kt) + ["ones_col"], writes=["P7_0"],
                        tag=tA + f"dn{qb}")
                fw.op("vector", (lambda jt=jt: v.reciprocal(
                    recipd[:, jt:jt + 1], PB7[:, jt:jt + 1])),
                    reads=["P7_0"], writes=["recipd"], tag=tA + "dnr")
                sri = jglob % 3
                gi = jglob % 3
                xri = jglob % 2
                fw.dma("sr", (lambda b=b, jglob=jglob, sri=sri: sy.dma_start(
                    dpair_s(sri), d_ssp[b * N + jglob * 128: b * N + jglob * 128 + 128, :])),
                    reads=[f"hs{b}_{jglob}"], writes=sr_bufs(sri), tag=tA + "srl")
                fw.dma("sr", (lambda b=b, jglob=jglob, gi=gi: sy.dma_start(
                    dpair_g(gi), d_gsp[b * N + jglob * 128: b * N + jglob * 128 + 128, :])),
                    reads=[f"hg{b}_{jglob}"], writes=gr_bufs(gi), tag=tA + "grl")
                fw.dma("xt", (lambda b=b, jglob=jglob, xri=xri: sy.dma_start(
                    dpair_x(xri), d_x_tok[b * N + jglob * 128: b * N + jglob * 128 + 128, :])),
                    writes=xr_bufs(xri), tag=tA + "xr2")
                oi = jglob % 2
                for dh in range(2):
                    pb = dh % 2
                    for kt in range(NT):
                        fw.op("tensor", (lambda kt=kt, jt=jt, dh=dh, pb=pb, pp=pp: t.matmul(
                            PB[pb][:], AT_ap(pp, kt, jt * 128, 128),
                            sbf_ap(kt, dh * 512, 512),
                            start=(kt == 0), stop=(kt == NT - 1))),
                            reads=AT_bufs(pp, kt) + sbf_bufs(kt), writes=[psb[pb]],
                            tag=tA + f"pv{qb}")
                    fw.op("vector", (lambda jt=jt, dh=dh, pb=pb, sri=sri: v.scalar_tensor_tensor(
                        MSC[:, 0:512], PB[pb][:], recipd[:, jt:jt + 1],
                        dpair_s(sri)[:, dh * 512:(dh + 1) * 512], ALU.mult, ALU.add)),
                        reads=[psb[pb], "recipd", sr_bufs(sri)[dh]],
                        writes=["D22a", "D22b"], tag=tA + f"cv{qb}")
                    fw.op("vector", (lambda dh=dh: v.tensor_add(
                        MSC[:, 0:512], MSC[:, 0:512], GB[:, dh * 512:(dh + 1) * 512])),
                        reads=["D22a", "D22b"] + [GB_bufs[dh]], writes=["D22a", "D22b"])
                    fw.op("scalar", (lambda dh=dh, oi=oi: sc.activation(
                        dpair_o(oi)[:, dh * 512:(dh + 1) * 512], MSC[:, 0:512], AF.Tanh,
                        scale=betacol[:, 0:1])),
                        reads=["D22a", "D22b", "betacol"], writes=[or_bufs(oi)[dh]],
                        tag=tA + "tanh")
                    fw.op("gpsimd", (lambda dh=dh, oi=oi, gi=gi: gp.tensor_mul(
                        dpair_o(oi)[:, dh * 512:(dh + 1) * 512],
                        dpair_o(oi)[:, dh * 512:(dh + 1) * 512],
                        dpair_g(gi)[:, dh * 512:(dh + 1) * 512])),
                        reads=[or_bufs(oi)[dh], gr_bufs(gi)[dh]], writes=[or_bufs(oi)[dh]])
                    fw.op("gpsimd", (lambda dh=dh, oi=oi, xri=xri: gp.tensor_add(
                        dpair_o(oi)[:, dh * 512:(dh + 1) * 512],
                        dpair_o(oi)[:, dh * 512:(dh + 1) * 512],
                        dpair_x(xri)[:, dh * 512:(dh + 1) * 512])),
                        reads=[or_bufs(oi)[dh], xr_bufs(xri)[dh]], writes=[or_bufs(oi)[dh]])
                fw.dma("gout", (lambda b=b, jglob=jglob, oi=oi: sc.dma_start(
                    d_out[b * N + jglob * 128: b * N + jglob * 128 + 128, :], dpair_o(oi))),
                    reads=or_bufs(oi), writes=[f"ho{b}_{jglob}"], tag=tA + "out")

    for b in range(NB):
        emit_batch(b)

    fw.op("sync", lambda: None,
          reads=[f"ho{b}_{j}" for b in range(NB) for j in range(NT)], tag="final")
    fw.ops[-1]["final"] = True

    with nc.Block() as block:
        fw.emit(block, sems)

    nc._ctx_keepalive = ctx
    return nc


_NC_CACHE = None


def kernel(L0_attn_x, L0_attn_W_spin, L0_attn_b_spin, L0_attn_W_global, L0_attn_W_J,
           L0_attn_beta, L0_ffn_centroids, L0_ffn_expert_biases, L0_ffn_Wr, L0_ffn_br,
           L0_ffn_Ws, L0_ffn_bs):
    global _NC_CACHE
    x = np.asarray(L0_attn_x, dtype=np.float32)

    WrT = np.ascontiguousarray(np.asarray(L0_ffn_Wr, np.float32).transpose(0, 2, 1)).reshape(NE * D, D)
    WsT = np.ascontiguousarray(np.asarray(L0_ffn_Ws, np.float32).transpose(0, 2, 1)).reshape(2 * D, D)
    WspinT = np.ascontiguousarray(np.asarray(L0_attn_W_spin, np.float32).T)
    WJT = np.ascontiguousarray(np.asarray(L0_attn_W_J, np.float32).T)
    WglobT = np.ascontiguousarray(np.asarray(L0_attn_W_global, np.float32).T)
    centT = np.ascontiguousarray(np.asarray(L0_ffn_centroids, np.float32).T)
    br = np.asarray(L0_ffn_br, np.float32)
    bspinB = np.ascontiguousarray(np.broadcast_to(
        np.asarray(L0_attn_b_spin, np.float32)[None, :], (128, D)))
    bs = np.asarray(L0_ffn_bs, np.float32)
    bsB2 = np.ascontiguousarray(np.broadcast_to(bs[:, None, :], (2, 128, D))).reshape(256, D)
    ebB = np.ascontiguousarray(np.broadcast_to(
        np.asarray(L0_ffn_expert_biases, np.float32)[None, :], (128, NE)))
    betacol = np.ascontiguousarray(np.broadcast_to(
        np.asarray(L0_attn_beta, np.float32).reshape(1, 1), (128, 1)))
    identbf = np.eye(128, dtype=ml_dtypes.bfloat16)
    ones_col = np.ones((128, 1), dtype=ml_dtypes.bfloat16)
    ones_row = np.ones((1, 128), dtype=ml_dtypes.bfloat16)
    ones_rowf = np.ones((1, 128), dtype=np.float32)

    if _NC_CACHE is None:
        _NC_CACHE = build_nc()
    nc = _NC_CACHE

    in_maps = []
    for cidx in range(NCORES):
        xs = x[cidx * NB:(cidx + 1) * NB]
        x_tok = np.ascontiguousarray(xs.reshape(NB * N, D))
        x_dm = np.ascontiguousarray(xs.transpose(0, 2, 1).reshape(NB * D, N))
        in_maps.append({
            "x_tok": x_tok, "x_dm": x_dm, "WrT": WrT, "WsT": WsT,
            "WspinT": WspinT, "WJT": WJT, "WglobT": WglobT, "centT": centT,
            "br": br, "bspinB": bspinB, "bsB2": bsB2, "ebB": ebB,
            "betacol": betacol, "identbf": identbf, "ones_col": ones_col,
            "ones_row": ones_row, "ones_rowf": ones_rowf,
        })

    res = run_bass_kernel_spmd(nc, in_maps, core_ids=list(range(NCORES)))
    outs = [np.asarray(r["out"]).reshape(NB, N, D) for r in res.results]
    return np.concatenate(outs, axis=0)


# version 17
# speedup vs baseline: 1.0670x; 1.0670x over previous
import numpy as np
import ml_dtypes
import concourse.bass as bass
import concourse.mybir as mybir
from concourse.bass_utils import run_bass_kernel_spmd

F32 = mybir.dt.float32
BF16 = mybir.dt.bfloat16
AF = mybir.ActivationFunctionType
ALU = mybir.AluOpType
AX = mybir.AxisListType

NCORES = 8
NB = 4
N = 2048
D = 1024
NCH = 8
NT = 16
NE = 16
TOPK = 2
EPS = 1e-5
SCL = 1.0 / 32.0

ENGS = ("tensor", "scalar", "vector", "gpsimd", "sync")


class Fw:

    def __init__(self):
        self.ops = []
        self.streams = {}

    def op(self, eng, fn, reads=(), writes=(), tag=""):
        self.ops.append(dict(eng=eng, fn=fn, reads=tuple(reads), writes=tuple(writes),
                             dma=None, tag=tag, inc=False, deps=None))

    def dma_stream(self, name, queue, nslots, sem_alloc):
        self.streams[name] = dict(queue=queue, sems=[sem_alloc() for _ in range(nslots)],
                                  uses=[0] * nslots, n=0, nslots=nslots)

    def dma(self, stream, fn, reads=(), writes=(), tag=""):
        st = self.streams[stream]
        slot = st["n"] % st["nslots"]
        prior = st["uses"][slot]
        st["uses"][slot] = prior + 1
        st["n"] += 1
        self.ops.append(dict(eng=st["queue"], fn=fn, reads=tuple(reads), writes=tuple(writes),
                             dma=(stream, slot, prior), tag=tag, inc=False, deps=None))

    def analyze(self):
        last_write = {}
        readers = {}
        for i, o in enumerate(self.ops):
            deps = set()
            for b in o["reads"]:
                w = last_write.get(b)
                if w is not None:
                    deps.add(w)
            for b in o["writes"]:
                w = last_write.get(b)
                if w is not None:
                    deps.add(w)
                for r in readers.get(b, ()):
                    deps.add(r)
            deps.discard(i)
            fd = []
            for j in deps:
                oj = self.ops[j]
                if (oj["dma"] is None and o["dma"] is None
                        and oj["eng"] == "tensor" and o["eng"] == "tensor"):
                    continue
                fd.append(j)
            o["deps"] = sorted(fd)
            for j in o["deps"]:
                if self.ops[j]["dma"] is None:
                    self.ops[j]["inc"] = True
            for b in o["reads"]:
                readers.setdefault(b, []).append(i)
            for b in o["writes"]:
                last_write[b] = i
                readers[b] = [i]
        cnt = {e: 0 for e in ENGS}
        for o in self.ops:
            if o["dma"] is None:
                if o["inc"]:
                    cnt[o["eng"]] += 1
                o["cnt"] = cnt[o["eng"]]

    def emit(self, block, eng_sems):
        self.analyze()
        guar = {e: {} for e in ENGS}

        def signal(j):
            oj = self.ops[j]
            if oj["dma"] is None:
                return ("E", oj["eng"]), oj["cnt"]
            stream, slot, prior = oj["dma"]
            return ("D", stream, slot), 16 * (prior + 1)

        def sem_handle(key):
            if key[0] == "E":
                return eng_sems[key[1]]
            return self.streams[key[1]]["sems"][key[2]]

        def run_engine(eng_name, eng):
            g = guar[eng_name]
            for o in self.ops:
                if o["eng"] != eng_name:
                    continue
                if o.get("final"):
                    for stname, st in sorted(self.streams.items()):
                        for slot in range(st["nslots"]):
                            if st["uses"][slot] == 0:
                                continue
                            k = ("D", stname, slot)
                            vv = 16 * st["uses"][slot]
                            if g.get(k, 0) < vv:
                                eng.wait_ge(st["sems"][slot], vv)
                                g[k] = vv
                waits = {}
                for j in o["deps"]:
                    k, vv = signal(j)
                    if waits.get(k, 0) < vv:
                        waits[k] = vv
                if o["dma"] is not None:
                    stream, slot, prior = o["dma"]
                    if prior > 0:
                        k = ("D", stream, slot)
                        if waits.get(k, 0) < 16 * prior:
                            waits[k] = 16 * prior
                for k, vv in sorted(waits.items(), key=lambda kv: str(kv[0])):
                    if g.get(k, 0) >= vv:
                        continue
                    eng.wait_ge(sem_handle(k), vv)
                    g[k] = vv
                instr = o["fn"]()
                if o["dma"] is not None:
                    stream, slot, prior = o["dma"]
                    instr.then_inc(self.streams[stream]["sems"][slot], 16)
                elif o["inc"]:
                    instr.then_inc(eng_sems[o["eng"]], 1)

        for name in ENGS:
            dec = getattr(block, name)

            def make(nm):
                def body(e):
                    run_engine(nm, e)
                return body
            dec(make(name))


def build_nc():
    nc = bass.Bass("TRN2", target_bir_lowering=False)
    t, sc, v, gp, sy = nc.tensor, nc.scalar, nc.vector, nc.gpsimd, nc.sync

    d_x_tok = nc.dram_tensor("x_tok", [NB * N, D], F32, kind="ExternalInput").ap()
    d_x_dm = nc.dram_tensor("x_dm", [NB * D, N], F32, kind="ExternalInput").ap()
    d_WrT = nc.dram_tensor("WrT", [NE * D, D], F32, kind="ExternalInput").ap()
    d_WsT = nc.dram_tensor("WsT", [2 * D, D], F32, kind="ExternalInput").ap()
    d_WspinT = nc.dram_tensor("WspinT", [D, D], F32, kind="ExternalInput").ap()
    d_WJT = nc.dram_tensor("WJT", [D, D], F32, kind="ExternalInput").ap()
    d_WglobT = nc.dram_tensor("WglobT", [D, D], F32, kind="ExternalInput").ap()
    d_centT = nc.dram_tensor("centT", [D, NE], F32, kind="ExternalInput").ap()
    d_br = nc.dram_tensor("br", [NE, D], F32, kind="ExternalInput").ap()
    d_bspinB = nc.dram_tensor("bspinB", [128, D], F32, kind="ExternalInput").ap()
    d_bsB2 = nc.dram_tensor("bsB2", [2 * 128, D], F32, kind="ExternalInput").ap()
    d_ebB = nc.dram_tensor("ebB", [128, NE], F32, kind="ExternalInput").ap()
    d_betacol = nc.dram_tensor("betacol", [128, 1], F32, kind="ExternalInput").ap()
    d_ident = nc.dram_tensor("identbf", [128, 128], BF16, kind="ExternalInput").ap()
    d_ones_col = nc.dram_tensor("ones_col", [128, 1], BF16, kind="ExternalInput").ap()
    d_ones_row = nc.dram_tensor("ones_row", [1, 128], BF16, kind="ExternalInput").ap()
    d_ones_rowf = nc.dram_tensor("ones_rowf", [1, 128], F32, kind="ExternalInput").ap()

    d_out = nc.dram_tensor("out", [NB * N, D], F32, kind="ExternalOutput").ap()
    d_gsp = nc.dram_tensor("g_spill", [NB * N, D], F32, kind="Internal").ap()
    d_wc = nc.dram_tensor("w_cache", [6 * 128, 4 * 1024], BF16, kind="Internal").ap()
    d_ssp = nc.dram_tensor("s_spill", [NB * N, D], F32, kind="Internal").ap()

    import contextlib
    ctx = contextlib.ExitStack()

    def sb(name, shape, dt):
        return ctx.enter_context(nc.sbuf_tensor("sb_" + name, shape, dt))

    def psa(name, shape, dt):
        return ctx.enter_context(nc.psum_tensor(name, shape, dt))

    A = sb("A", [128, 16 * 1024], BF16)
    Bt = sb("Bt", [128, 16 * 1024], BF16)
    Ct = sb("Ct", [128, 16 * 1024], BF16)
    Dt = sb("Dt", [128, 24 * 512], F32)
    W = [sb(f"W{i}", [128, 4 * 1024], BF16) for i in range(4)]
    J = sb("J", [128, 8 * 512], BF16)

    bspinB = sb("bspinB", [128, D], F32)
    bsB = sb("bsB", [128, D], F32)
    ebB = sb("ebB", [128, NE], F32)
    betacol = sb("betacol", [128, 1], F32)
    identbf = sb("identbf", [128, 128], BF16)
    ones_col = sb("ones_col", [128, 1], BF16)
    ones_row = sb("ones_row", [1, 128], BF16)
    ones_rowf = sb("ones_rowf", [1, 128], F32)
    centT = sb("centT", [128, NCH * NE], BF16)
    br_bf = sb("br_bf", [16, D], BF16)
    gT_bf = sb("gT_bf", [16, N], BF16)
    m_sum = sb("m_sum", [128, NT], F32)
    q_sum = sb("q_sum", [128, NT], F32)
    m_col = sb("m_col", [128, NT], F32)
    r_col = sb("r_col", [128, NT], F32)
    rm_col = sb("rm_col", [128, NT], F32)
    halfr = sb("halfr", [128, NT], F32)
    st1 = sb("st1", [128, NT], F32)
    st2 = sb("st2", [128, NT], F32)
    g_r = sb("g_r", [128, NT * NE], F32)
    gr_r = sb("gr_r", [128, NT * NE], F32)
    g_bf = sb("g_bf", [128, NT * NE], BF16)
    top8 = sb("top8", [128, 8], F32)
    pmneg = sb("pmneg", [128, 1], F32)
    precip = sb("precip", [128, 1], F32)
    snorm2 = sb("snorm2", [128, NT], F32)
    smax2c = sb("smax2c", [128, 1], F32)
    smax2bf = sb("smax2bf", [128, 1], BF16)
    cval = sb("cval", [1, 2], F32)
    cneg = sb("cneg", [128, 1], F32)
    recipd = sb("recipd", [128, 4], F32)
    mf_bf = sb("mf_bf", [128, NCH], BF16)
    mfs = sb("mfs", [128, 1], F32)

    PB = [psa(f"PB{i}", [128, 512], F32) for i in range(6)]
    PBb = psa("PBb", [128, 1024], BF16)
    PB7 = psa("PB7", [128, 512], F32)
    psb = {0: "PB0", 1: "PB1", 2: "PB2", 3: "PB3", 4: "PB4", 5: "PB5", 7: "PB7"}
    P7ALL = [f"P7_{u}" for u in range(32)]

    fw = Fw()
    sems = {e: ctx.enter_context(nc.semaphore(f"es_{e}")) for e in ENGS}

    _semn = [0]

    def mksem():
        _semn[0] += 1
        return ctx.enter_context(nc.semaphore(f"dsem{_semn[0]}"))

    fw.dma_stream("xcl", "gpsimd", 2, mksem)
    fw.dma_stream("wl", "gpsimd", 4, mksem)
    fw.dma_stream("cl", "gpsimd", 2, mksem)
    fw.dma_stream("xt", "sync", 3, mksem)
    fw.dma_stream("sw", "scalar", 4, mksem)
    fw.dma_stream("sr", "sync", 4, mksem)
    fw.dma_stream("gout", "scalar", 4, mksem)
    fw.dma_stream("cst", "sync", 2, mksem)
    fw.dma_stream("wc", "scalar", 2, mksem)
    fw.dma_stream("wls", "sync", 2, mksem)
    fw.dma_stream("wlc", "scalar", 2, mksem)

    def aid(lo, hi):
        return [f"A{u}" for u in range(lo, hi)]

    def did(u):
        return f"D{u}"

    def wid(slot):
        return f"W{slot}"

    def xcT(c, col0, w):
        return A[:, c * 2048 + col0: c * 2048 + col0 + w]

    def xcT_bufs(c, col0, w):
        lo = (c * 2048 + col0) // 1024
        hi = (c * 2048 + col0 + w + 1023) // 1024
        return aid(lo, hi)

    def AT_ap(pp, kt, q0, w):
        base = kt * 512 + q0
        return A[:, base:base + w]

    def AT_bufs(pp, kt):
        u = (kt * 512) // 1024
        return aid(u, u + 1)

    def sbf_ap(j, col0, w):
        return Bt[:, j * 1024 + col0: j * 1024 + col0 + w]

    def sbf_bufs(j):
        return [f"B{j}"]

    def sT_ap(c, col0, w):
        return Ct[:, c * 2048 + col0: c * 2048 + col0 + w]

    def sT_bufs(c):
        return [f"C{2 * c}", f"C{2 * c + 1}"]

    def routed_ap(j, col0, w):
        tens = Bt if j < 8 else Ct
        jj = j if j < 8 else j - 8
        return tens[:, jj * 2048: jj * 2048 + 2048].bitcast(F32)[:, col0:col0 + w]

    def routed_bufs(j):
        if j < 8:
            return [f"B{2 * j}", f"B{2 * j + 1}"]
        return [f"C{2 * (j - 8)}", f"C{2 * (j - 8) + 1}"]

    def eT_ap(c, w=512):
        return J[:, c * 512: c * 512 + w]

    def eT_bufs(c):
        return [f"J{c}"]

    def dpair_x(i):
        return Dt[:, (2 * i) * 512:(2 * i + 2) * 512]

    def xr_bufs(i):
        return [did(2 * i), did(2 * i + 1)]

    def dpair_g(i):
        return Dt[:, (4 + 2 * i) * 512:(6 + 2 * i) * 512]

    def gr_bufs(i):
        return [did(4 + 2 * i), did(5 + 2 * i)]

    def dpair_s(i):
        return Dt[:, (10 + 2 * i) * 512:(12 + 2 * i) * 512]

    def sr_bufs(i):
        return [did(10 + 2 * i), did(11 + 2 * i)]

    def dpair_o(i):
        return Dt[:, (16 + 2 * i) * 512:(18 + 2 * i) * 512]

    def or_bufs(i):
        return [did(16 + 2 * i), did(17 + 2 * i)]

    GB = Dt[:, 20 * 512:22 * 512]
    GB_bufs = [did(20), did(21)]
    MSC = Dt[:, 22 * 512:23 * 512]
    MSC_bufs = [did(22)]
    mrow_bf_ap = Dt[0:1, 23 * 512:23 * 512 + 256].bitcast(BF16)
    ROW_bufs = [did(23)]
    sm_row = Dt[0:1, 23 * 512 + 256:23 * 512 + 384]
    grow = Dt[0:1, 20 * 512:20 * 512 + 1024]
    GROW_bufs = [did(20), did(21)]

    fw.dma("cst", lambda: sy.dma_start(bspinB[:], d_bspinB), writes=["bspinB"])
    fw.dma("cst", lambda: sy.dma_start(ebB[:], d_ebB), writes=["ebB"])
    fw.dma("cst", lambda: sy.dma_start(betacol[:], d_betacol), writes=["betacol"])
    fw.dma("cst", lambda: sy.dma_start(identbf[:], d_ident), writes=["identbf"])
    fw.dma("cst", lambda: sy.dma_start(ones_col[:], d_ones_col), writes=["ones_col"])
    fw.dma("cst", lambda: sy.dma_start(ones_row[:], d_ones_row), writes=["ones_row"])
    fw.dma("cst", lambda: sy.dma_start(ones_rowf[:], d_ones_rowf), writes=["ones_rowf"])
    fw.dma("cst", lambda: sy.dma_start(dpair_x(0), d_bsB2[0:128, :]), writes=xr_bufs(0))
    fw.dma("cst", lambda: sy.dma_start(dpair_g(0), d_bsB2[128:256, :]), writes=gr_bufs(0))
    fw.op("vector", lambda: v.tensor_add(bsB[:], dpair_x(0), dpair_g(0)),
          reads=xr_bufs(0) + gr_bufs(0), writes=["bsB"])
    fw.op("vector", lambda: v.tensor_scalar_mul(bsB[:], bsB[:], 0.5),
          reads=["bsB"], writes=["bsB"])
    fw.dma("cl", lambda: gp.dma_start(
        centT[:].rearrange("p (c e) -> p c e", e=NE),
        d_centT.rearrange("(c p) e -> p c e", p=128)), writes=["centT"])
    fw.dma("cl", lambda: gp.dma_start(br_bf[:], d_br), writes=["br_bf"])

    def load_w_half(slot, dram, row0, tag):
        def fn(sl=slot, dr=dram, r0=row0):
            return gp.dma_start(
                W[sl][:].rearrange("p (c o) -> p c o", o=D),
                dr[r0:r0 + 512, :].rearrange("(c p) o -> p c o", p=128))
        fw.dma("wl", fn, writes=[wid(slot)], tag=tag)

    def load_w_cached(slot, dram, row0, cidx, first, stream, tag):
        if first:
            load_w_half(slot, dram, row0, tag)
            eng = sc if fw.streams["wc"]["queue"] == "scalar" else sy
            fw.dma("wc", (lambda sl=slot, ci=cidx, eng=eng: eng.dma_start(
                d_wc[ci * 128:(ci + 1) * 128, :], W[sl][:])),
                reads=[wid(slot)], writes=[f"wc{cidx}"], tag=tag + "st")
        else:
            qn = fw.streams[stream]["queue"]
            eng = {"scalar": sc, "sync": sy, "gpsimd": gp}[qn]
            fw.dma(stream, (lambda sl=slot, ci=cidx, eng=eng: eng.dma_start(
                W[sl][:], d_wc[ci * 128:(ci + 1) * 128, :])),
                reads=[f"wc{cidx}"], writes=[wid(slot)], tag=tag)

    def w_rhs(pair, c, oh):
        sl = pair[c // 4]
        base = (c % 4) * 1024 + oh * 512
        return W[sl][:, base:base + 512]

    def w_sl(pair, c, col0, w):
        sl = pair[c // 4]
        base = (c % 4) * 1024 + col0
        return W[sl][:, base:base + w]

    wl_pairs = [(0, 1), (2, 3)]

    def emit_xcl_half(b, half):
        fw.dma("xcl", (lambda b=b, half=half: gp.dma_start(
            A[:, half * 8192:(half + 1) * 8192].rearrange("p (c n) -> p c n", n=N),
            d_x_dm[b * D + half * 512: b * D + (half + 1) * 512, :]
            .rearrange("(c p) n -> p c n", p=128))),
            writes=aid(8 * half, 8 * half + 8), tag=f"b{b}_xcl{half}")

    def emit_batch(b):
        tA = f"b{b}_"

        emit_xcl_half(b, 0)
        emit_xcl_half(b, 1)

        for c in range(NCH):
            fw.op("scalar", (lambda c=c: sc.activation(
                J[:, 0:2048], xcT(c, 0, 2048), AF.Square)),
                reads=xcT_bufs(c, 0, 2048), writes=["J0", "J1", "J2", "J3"],
                tag=tA + "sq")
            for j in range(NT):
                fw.op("tensor", (lambda c=c, j=j: t.matmul(
                    PB[2][:, j:j + 1], xcT(c, j * 128, 128), ones_col[:],
                    start=True, stop=True)),
                    reads=xcT_bufs(c, j * 128, 128) + ["ones_col"], writes=["PB2"],
                    tag=tA + "mc")
                fw.op("tensor", (lambda c=c, j=j: t.matmul(
                    PB[3][:, j:j + 1], J[:, j * 128:(j + 1) * 128], ones_col[:],
                    start=True, stop=True)),
                    reads=["J0", "J1", "J2", "J3", "ones_col"], writes=["PB3"],
                    tag=tA + "qc")
            if c == 0:
                fw.op("vector", (lambda: v.tensor_copy(m_sum[:], PB[2][:, 0:NT])),
                      reads=["PB2"], writes=["m_sum"], tag=tA + "mca")
                fw.op("vector", (lambda: v.tensor_copy(q_sum[:], PB[3][:, 0:NT])),
                      reads=["PB3"], writes=["q_sum"], tag=tA + "qca")
            else:
                fw.op("vector", (lambda: v.tensor_add(m_sum[:], m_sum[:], PB[2][:, 0:NT])),
                      reads=["m_sum", "PB2"], writes=["m_sum"], tag=tA + "mca")
                fw.op("vector", (lambda: v.tensor_add(q_sum[:], q_sum[:], PB[3][:, 0:NT])),
                      reads=["q_sum", "PB3"], writes=["q_sum"], tag=tA + "qca")

        for quad in range(4):
            for jj in range(4):
                j = quad * 4 + jj
                for c in range(NCH):
                    fw.op("tensor", (lambda c=c, j=j, jj=jj: t.matmul(
                        PB[3][0:1, jj * 128:jj * 128 + 128], ones_col[:],
                        xcT(c, j * 128, 128), start=(c == 0), stop=(c == NCH - 1))),
                        reads=["ones_col"] + xcT_bufs(c, j * 128, 128), writes=["PB3"],
                        tag=tA + "mrow")
            fw.op("scalar", (lambda: sc.activation(mrow_bf_ap, PB[3][0:1, :], AF.Copy,
                                                   scale=1.0 / D)),
                  reads=["PB3"], writes=ROW_bufs, tag=tA + "mrowe")
            fw.op("tensor", (lambda: t.matmul(PB[2][:], ones_row[:], mrow_bf_ap,
                                              start=True, stop=True)),
                  reads=["ones_row"] + ROW_bufs, writes=["PB2"], tag=tA + "mB")
            for c in range(NCH):
                fw.op("vector", (lambda c=c, quad=quad: v.tensor_sub(
                    xcT(c, quad * 512, 512), xcT(c, quad * 512, 512), PB[2][:])),
                    reads=xcT_bufs(c, quad * 512, 512) + ["PB2"],
                    writes=xcT_bufs(c, quad * 512, 512), tag=tA + "ctr")

        fw.op("vector", lambda: v.tensor_scalar_mul(m_col[:], m_sum[:], 1.0 / D),
              reads=["m_sum"], writes=["m_col"])
        fw.op("vector", lambda: v.tensor_scalar_mul(st1[:], q_sum[:], 1.0 / D),
              reads=["q_sum"], writes=["st1"])
        fw.op("vector", lambda: v.tensor_mul(st2[:], m_col[:], m_col[:]),
              reads=["m_col"], writes=["st2"])
        fw.op("vector", lambda: v.tensor_sub(st1[:], st1[:], st2[:]),
              reads=["st1", "st2"], writes=["st1"])
        fw.op("vector", lambda: v.tensor_scalar_add(st1[:], st1[:], EPS),
              reads=["st1"], writes=["st1"])
        fw.op("scalar", lambda: sc.activation(st2[:], st1[:], AF.Sqrt),
              reads=["st1"], writes=["st2"])
        fw.op("vector", lambda: v.reciprocal(r_col[:], st2[:]),
              reads=["st2"], writes=["r_col"])
        fw.op("vector", lambda: v.tensor_mul(st2[:], r_col[:], r_col[:]),
              reads=["r_col"], writes=["st2"])
        fw.op("vector", lambda: v.tensor_mul(st2[:], st2[:], st1[:]),
              reads=["st2", "st1"], writes=["st2"])
        fw.op("vector", lambda: v.tensor_scalar(st2[:], st2[:], -0.5, 1.5, ALU.mult, ALU.add),
              reads=["st2"], writes=["st2"])
        fw.op("vector", lambda: v.tensor_mul(r_col[:], r_col[:], st2[:]),
              reads=["r_col", "st2"], writes=["r_col"])
        fw.op("vector", lambda: v.tensor_mul(rm_col[:], r_col[:], m_col[:]),
              reads=["r_col", "m_col"], writes=["rm_col"])
        fw.op("vector", lambda: v.tensor_scalar_mul(halfr[:], r_col[:], 0.5),
              reads=["r_col"], writes=["halfr"])

        for j in range(NT):
            for c in range(NCH):
                fw.op("tensor", (lambda c=c, j=j: t.matmul(
                    PB7[:, j * NE:(j + 1) * NE], xcT(c, j * 128, 128), centT[:, c * NE:(c + 1) * NE],
                    start=(c == 0), stop=(c == NCH - 1))),
                    reads=xcT_bufs(c, j * 128, 128) + ["centT"], writes=[f"P7_{j}"],
                    tag=tA + "rt")
        for j in range(NT):
            fw.op("vector", (lambda j=j: v.tensor_scalar(
                g_r[:, j * NE:(j + 1) * NE], PB7[:, j * NE:(j + 1) * NE], r_col[:, j:j + 1], None, ALU.mult)),
                reads=[f"P7_{j}", "r_col"], writes=["g_r"], tag=tA + "rte")
            fw.op("vector", (lambda j=j: v.tensor_add(
                g_r[:, j * NE:(j + 1) * NE], g_r[:, j * NE:(j + 1) * NE], ebB[:])),
                reads=["g_r", "ebB"], writes=["g_r"])
            fw.op("vector", (lambda j=j: v.reduce_max(
                pmneg[:], g_r[:, j * NE:(j + 1) * NE], axis=AX.X)),
                reads=["g_r"], writes=["pmneg"])
            fw.op("vector", lambda: v.tensor_scalar_mul(pmneg[:], pmneg[:], -1.0),
                  reads=["pmneg"], writes=["pmneg"])
            fw.op("scalar", (lambda j=j: sc.activation(
                g_r[:, j * NE:(j + 1) * NE], g_r[:, j * NE:(j + 1) * NE], AF.Exp,
                bias=pmneg[:, 0:1], accum_out=precip[:, 0:1])),
                reads=["g_r", "pmneg"], writes=["g_r", "precip"], tag=tA + "rtx")
            fw.op("vector", lambda: v.reciprocal(precip[:], precip[:]),
                  reads=["precip"], writes=["precip"])
            fw.op("vector", (lambda j=j: v.tensor_scalar(
                g_r[:, j * NE:(j + 1) * NE], g_r[:, j * NE:(j + 1) * NE],
                precip[:, 0:1], None, ALU.mult)),
                reads=["g_r", "precip"], writes=["g_r"])
            fw.op("vector", (lambda j=j: v.max(
                out=top8[:], in_=g_r[:, j * NE:(j + 1) * NE])),
                reads=["g_r"], writes=["top8"])
            fw.op("vector", (lambda j=j: v.tensor_scalar(
                st1[:, 0:NE], g_r[:, j * NE:(j + 1) * NE], top8[:, TOPK - 1:TOPK],
                None, ALU.is_ge)),
                reads=["g_r", "top8"], writes=["st1"])
            fw.op("vector", (lambda j=j: v.tensor_mul(
                g_r[:, j * NE:(j + 1) * NE], g_r[:, j * NE:(j + 1) * NE], st1[:, 0:NE])),
                reads=["g_r", "st1"], writes=["g_r"])
            fw.op("vector", (lambda j=j: v.tensor_scalar(
                gr_r[:, j * NE:(j + 1) * NE], g_r[:, j * NE:(j + 1) * NE],
                r_col[:, j:j + 1], None, ALU.mult)),
                reads=["g_r", "r_col"], writes=["gr_r"])
            fw.op("vector", (lambda j=j: v.tensor_copy(
                g_bf[:, j * NE:(j + 1) * NE], g_r[:, j * NE:(j + 1) * NE])),
                reads=["g_r"], writes=[f"gbf{j}"])

        def emit_gT_br(j):
            fw.op("tensor", (lambda j=j: t.transpose(
                PBb[0:NE, 0:128], g_bf[:, j * NE:(j + 1) * NE], identbf[:])),
                reads=[f"gbf{j}", "identbf"], writes=["PBb"], tag=tA + "gT")
            fw.op("scalar", (lambda j=j: sc.activation(
                gT_bf[0:16, j * 128:(j + 1) * 128], PBb[0:NE, 0:128], AF.Copy)),
                reads=["PBb"], writes=[f"gTb{j}"])
            for oh in range(2):
                pbk = 4 + (j * 2 + oh) % 2
                pbap = PB[4] if pbk == 4 else PB[5]
                mid = GB_bufs[oh]
                fw.op("tensor", (lambda j=j, oh=oh, pbap=pbap: t.matmul(
                    pbap[:], gT_bf[0:16, j * 128:(j + 1) * 128],
                    br_bf[0:16, oh * 512:(oh + 1) * 512], start=True, stop=True)),
                    reads=[f"gTb{j}", "br_bf"], writes=[psb[pbk]], tag=tA + "br")
                fw.op("vector", (lambda j=j, oh=oh, pbap=pbap: v.scalar_tensor_tensor(
                    GB[:, oh * 512:(oh + 1) * 512], pbap[:], rm_col[:, j:j + 1],
                    bsB[:, oh * 512:(oh + 1) * 512], ALU.subtract, ALU.add)),
                    reads=[psb[pbk], "rm_col", "bsB"], writes=[mid],
                    tag=tA + "brv")
                fw.op("vector", (lambda j=j, oh=oh: v.tensor_add(
                    routed_ap(j, oh * 512, 512), routed_ap(j, oh * 512, 512),
                    GB[:, oh * 512:(oh + 1) * 512])),
                    reads=[mid] + routed_bufs(j), writes=routed_bufs(j),
                    tag=tA + "brw")

        e_order = [NE] + list(range(NE)) + [NE + 1]
        for ei, e in enumerate(e_order):
            pair = wl_pairs[ei % 2]
            dram = d_WrT if e < NE else d_WsT
            row = e * D if e < NE else (e - NE) * D
            load_w_half(pair[0], dram, row, tA + f"w{e}a")
            load_w_half(pair[1], dram, row + 512, tA + f"w{e}b")
            for j in range(NT):
                pb0 = 2 * (j % 2)
                for c in range(NCH):
                    for oh in range(2):
                        fw.op("tensor", (lambda c=c, j=j, oh=oh, pb=pb0 + oh, pair=pair: t.matmul(
                            PB[pb][:], xcT(c, j * 128, 128), w_rhs(pair, c, oh),
                            start=(c == 0), stop=(c == NCH - 1))),
                            reads=xcT_bufs(c, j * 128, 128) + [wid(pair[c // 4])],
                            writes=[psb[pb0 + oh]], tag=tA + f"e{e}")
                for oh in range(2):
                    pb = pb0 + oh
                    if e < NE:
                        scal = gr_r[:, j * NE + e:j * NE + e + 1]
                        srd = "gr_r"
                    else:
                        scal = halfr[:, j:j + 1]
                        srd = "halfr"
                    if ei == 0:
                        fw.op("vector", (lambda j=j, oh=oh, pb=pb, scal=scal: v.tensor_scalar(
                            routed_ap(j, oh * 512, 512), PB[pb][:], scal, None, ALU.mult)),
                            reads=[psb[pb], srd], writes=routed_bufs(j),
                            tag=tA + f"ev{e}")
                    else:
                        fw.op("vector", (lambda j=j, oh=oh, pb=pb, scal=scal: v.scalar_tensor_tensor(
                            routed_ap(j, oh * 512, 512), PB[pb][:], scal,
                            routed_ap(j, oh * 512, 512), ALU.mult, ALU.add)),
                            reads=[psb[pb], srd] + routed_bufs(j),
                            writes=routed_bufs(j), tag=tA + f"ev{e}")
                if ei == 0 and j >= 2:
                    emit_gT_br(j - 2)
            if ei == 0:
                emit_gT_br(NT - 2)
                emit_gT_br(NT - 1)

        first = (b == 0)
        load_w_cached(0, d_WspinT, 0, 0, first, "wls", tA + "spA")
        load_w_cached(1, d_WspinT, 512, 1, first, "wls", tA + "spB")
        load_w_cached(2, d_WJT, 0, 2, first, "wlc", tA + "wjA")
        load_w_cached(3, d_WJT, 512, 3, first, "wlc", tA + "wjB")

        for j in range(NT):
            xr = j % 2
            gi = j % 3
            fw.dma("xt", (lambda b=b, j=j, xr=xr: sy.dma_start(
                dpair_x(xr), d_x_tok[b * N + j * 128: b * N + j * 128 + 128, :])),
                writes=xr_bufs(xr), tag=tA + "xr1")
            fw.op("vector", (lambda j=j, xr=xr, gi=gi: v.scalar_tensor_tensor(
                dpair_g(gi), dpair_x(xr), r_col[:, j:j + 1], routed_ap(j, 0, 1024),
                ALU.mult, ALU.add)),
                reads=xr_bufs(xr) + ["r_col"] + routed_bufs(j), writes=gr_bufs(gi),
                tag=tA + "xn")
            fw.op("scalar", (lambda gi=gi: sc.activation(
                dpair_g(gi), dpair_g(gi), AF.Silu)),
                reads=gr_bufs(gi), writes=gr_bufs(gi), tag=tA + "silu")
            fw.dma("sw", (lambda b=b, j=j, gi=gi: sc.dma_start(
                d_gsp[b * N + j * 128: b * N + j * 128 + 128, :], dpair_g(gi))),
                reads=gr_bufs(gi), writes=[f"hg{b}_{j}"], tag=tA + "gsp")

        spair = (0, 1)
        jpair = (2, 3)
        def emit_sT(j):
            for c in range(NCH):
                fw.op("tensor", (lambda j=j, c=c: t.transpose(
                    PBb[:, c * 128:(c + 1) * 128], sbf_ap(j, c * 128, 128), identbf[:])),
                    reads=sbf_bufs(j) + ["identbf"], writes=["PBb"], tag=tA + "sT")
            for c in range(NCH):
                fw.op("vector", (lambda j=j, c=c: v.tensor_copy(
                    sT_ap(c, j * 128, 128), PBb[:, c * 128:(c + 1) * 128])),
                    reads=["PBb"], writes=sT_bufs(c), tag=tA + "sTe")

        for j in range(NT):
            ss = j % 3
            pb0 = 2 * (j % 2)
            for c in range(NCH):
                for oh in range(2):
                    fw.op("tensor", (lambda c=c, j=j, oh=oh, pb=pb0 + oh: t.matmul(
                        PB[pb][:], xcT(c, j * 128, 128), w_rhs(spair, c, oh),
                        start=(c == 0), stop=(c == NCH - 1))),
                        reads=xcT_bufs(c, j * 128, 128) + [wid(spair[c // 4])],
                        writes=[psb[pb0 + oh]], tag=tA + "sm")
            for oh in range(2):
                pb = pb0 + oh
                fw.op("vector", (lambda j=j, oh=oh, pb=pb, ss=ss: v.scalar_tensor_tensor(
                    dpair_s(ss)[:, oh * 512:(oh + 1) * 512], PB[pb][:],
                    r_col[:, j:j + 1], bspinB[:, oh * 512:(oh + 1) * 512],
                    ALU.mult, ALU.add)),
                    reads=[psb[pb], "r_col", "bspinB"], writes=[sr_bufs(ss)[oh]],
                    tag=tA + "se")
            fw.op("scalar", (lambda j=j, ss=ss: sc.activation(
                sbf_ap(j, 0, 1024), dpair_s(ss), AF.Copy)),
                reads=sr_bufs(ss), writes=sbf_bufs(j), tag=tA + "sc")
            fw.op("vector", (lambda j=j: v.scalar_tensor_tensor(
                J[:, 2048:3072], sbf_ap(j, 0, 1024), 1.0, sbf_ap(j, 0, 1024),
                ALU.mult, ALU.mult, accum_out=snorm2[:, j:j + 1])),
                reads=sbf_bufs(j), writes=["J4", "J5", "snorm2"],
                tag=tA + "sn")
            fw.dma("sw", (lambda b=b, j=j, ss=ss: sc.dma_start(
                d_ssp[b * N + j * 128: b * N + j * 128 + 128, :], dpair_s(ss))),
                reads=sr_bufs(ss), writes=[f"hs{b}_{j}"], tag=tA + "ssp")
            if j > 0:
                emit_sT(j - 1)
        emit_sT(NT - 1)

        fw.op("vector", lambda: v.reduce_max(smax2c[:], snorm2[:], axis=AX.X),
              reads=["snorm2"], writes=["smax2c"], tag=tA + "sm2")
        fw.op("vector", lambda: v.tensor_copy(smax2bf[:], smax2c[:]),
              reads=["smax2c"], writes=["smax2bf"])
        fw.op("tensor", lambda: t.transpose(PBb[0:1, 0:128], smax2bf[:], identbf[:]),
              reads=["smax2bf", "identbf"], writes=["PBb"], tag=tA + "smT")
        fw.op("vector", lambda: v.tensor_copy(sm_row, PBb[0:1, 0:128]),
              reads=["PBb"], writes=ROW_bufs)
        fw.op("vector", lambda: v.reduce_max(cval[:, 0:1], sm_row, axis=AX.X),
              reads=ROW_bufs, writes=["cval"])

        load_w_cached(0, d_WglobT, 0, 4, first, "wls", tA + "wgA")
        load_w_cached(1, d_WglobT, 512, 5, first, "wls", tA + "wgB")

        def emit_mfglob():
            gpair = (0, 1)
            for oh in range(2):
                for kt in range(NT):
                    fw.op("tensor", (lambda kt=kt, oh=oh: t.matmul(
                        PB[3][0:1, 0:512], ones_col[:], sbf_ap(kt, oh * 512, 512),
                        start=(kt == 0), stop=(kt == NT - 1))),
                        reads=["ones_col"] + sbf_bufs(kt), writes=["PB3"],
                        tag=tA + "mfm")
                fw.op("scalar", (lambda oh=oh: sc.activation(
                    mrow_bf_ap, PB[3][0:1, 0:512], AF.Copy, scale=1.0 / N)),
                    reads=["PB3"], writes=ROW_bufs, tag=tA + "mfe")
                for c4 in range(4):
                    c = oh * 4 + c4
                    fw.op("tensor", (lambda c=c, c4=c4: t.transpose(
                        PBb[:, c4 * 128:c4 * 128 + 1],
                        mrow_bf_ap[0:1, c4 * 128:(c4 + 1) * 128], identbf[0:1, 0:1])),
                        reads=ROW_bufs + ["identbf"], writes=["PBb"], tag=tA + "mfT")
                    fw.op("vector", (lambda c=c, c4=c4: v.tensor_copy(
                        mf_bf[:, c:c + 1], PBb[:, c4 * 128:c4 * 128 + 1])),
                        reads=["PBb"], writes=["mf_bf"], tag=tA + "mfTe")
            for oh in range(2):
                for c in range(NCH):
                    fw.op("tensor", (lambda c=c, oh=oh: t.matmul(
                        PB[3][0:1, 0:512], mf_bf[:, c:c + 1], w_rhs(gpair, c, oh),
                        start=(c == 0), stop=(c == NCH - 1))),
                        reads=["mf_bf", wid(gpair[c // 4])], writes=["PB3"], tag=tA + "gl")
                fw.op("scalar", (lambda oh=oh: sc.activation(
                    grow[0:1, oh * 512:(oh + 1) * 512], PB[3][0:1, 0:512], AF.Copy)),
                    reads=["PB3"], writes=GROW_bufs, tag=tA + "gle")
            for oh in range(2):
                fw.op("tensor", (lambda oh=oh: t.matmul(
                    PB[3][:], ones_rowf[:], grow[0:1, oh * 512:(oh + 1) * 512],
                    start=True, stop=True)),
                    reads=["ones_rowf"] + GROW_bufs, writes=["PB3"], tag=tA + "gb")
                fw.op("vector", (lambda oh=oh: v.tensor_copy(
                    GB[:, oh * 512:(oh + 1) * 512], PB[3][:])),
                    reads=["PB3"], writes=[GB_bufs[oh]], tag=tA + "gbe")

        for qb in range(4):
            pp = qb % 2
            q0 = qb * 512
            def e2m_mm(c):
                half = c % 2
                fw.op("tensor", (lambda c=c, half=half: t.matmul(
                    PB[5][0:1, 0:512], ones_col[:],
                    MSC[:].bitcast(BF16)[:, half * 512:(half + 1) * 512],
                    start=(c == 0), stop=(c == NCH - 1))),
                    reads=["ones_col", f"D22{'ab'[half]}"], writes=["PB5"],
                    tag=tA + "e2m")

            for c in range(NCH):
                pbk = 4 if (c % 2 == 0) else 7
                pbap = PB[4] if pbk == 4 else PB7
                for c2 in range(NCH):
                    fw.op("tensor", (lambda c=c, c2=c2, pbap=pbap, q0=q0: t.matmul(
                        pbap[:], w_sl(jpair, c2, c * 128, 128), sT_ap(c2, q0, 512),
                        start=(c2 == 0), stop=(c2 == NCH - 1))),
                        reads=[wid(jpair[c2 // 4])] + sT_bufs(c2),
                        writes=(P7ALL if pbk == 7 else [psb[pbk]]), tag=tA + f"en{qb}")
                fw.op("scalar", (lambda c=c, pbap=pbap: sc.activation(
                    eT_ap(c), pbap[:], AF.Copy)),
                    reads=(P7ALL if pbk == 7 else [psb[pbk]]), writes=eT_bufs(c), tag=tA + f"ene{qb}")
                fw.op("scalar", (lambda c=c, half=c % 2: sc.activation(
                    MSC[:].bitcast(BF16)[:, half * 512:(half + 1) * 512],
                    eT_ap(c), AF.Square)),
                    reads=eT_bufs(c), writes=[f"D22{'ab'[c % 2]}"], tag=tA + "e2")
                if c >= 2:
                    e2m_mm(c - 2)
            e2m_mm(NCH - 2)
            e2m_mm(NCH - 1)
            def qk_group(kt):
                pb = 2 + (kt % 2)
                for c in range(NCH):
                    fw.op("tensor", (lambda c=c, kt=kt, pb=pb, q0=q0: t.matmul(
                        PB[pb][:], sT_ap(c, kt * 128, 128), eT_ap(c),
                        start=(c == 0), stop=(c == NCH - 1))),
                        reads=sT_bufs(c) + eT_bufs(c), writes=[psb[pb]],
                        tag=tA + f"qk{qb}")

            def exp_group(kt):
                pb = 2 + (kt % 2)
                fw.op("scalar", (lambda kt=kt, pb=pb, pp=pp: sc.activation(
                    AT_ap(pp, kt, 0, 512), PB[pb][:], AF.Exp, scale=SCL,
                    bias=cneg[:, 0:1])),
                    reads=[psb[pb], "cneg"], writes=AT_bufs(pp, kt), tag=tA + f"xp{qb}")

            qk_group(0)
            qk_group(1)
            fw.op("vector", lambda: v.reduce_max(cval[:, 1:2], PB[5][0:1, 0:512], axis=AX.X),
                  reads=["PB5"], writes=["cval"], tag=tA + "emx")
            fw.op("vector", lambda: v.tensor_add(cval[:, 1:2], cval[:, 1:2], cval[:, 0:1]),
                  reads=["cval"], writes=["cval"])
            fw.op("vector", lambda: v.tensor_scalar(
                cval[:, 1:2], cval[:, 1:2], -1.02 / 64.0, None, ALU.mult),
                reads=["cval"], writes=["cval"])
            fw.op("tensor", lambda: t.matmul(
                PB[5][:, 0:1], ones_rowf[:], cval[0:1, 1:2], start=True, stop=True),
                reads=["ones_rowf", "cval"], writes=["PB5"], tag=tA + "cb")
            fw.op("vector", lambda: v.tensor_copy(cneg[:], PB[5][:, 0:1]),
                  reads=["PB5"], writes=["cneg"])
            exp_group(0)
            exp_group(1)
            for kt in range(2, NT):
                qk_group(kt)
                exp_group(kt)
            if qb == 0:
                emit_mfglob()

            for jt in range(4):
                jglob = qb * 4 + jt
                for kt in range(NT):
                    fw.op("tensor", (lambda kt=kt, jt=jt, pp=pp: t.matmul(
                        PB7[:, jt:jt + 1], AT_ap(pp, kt, jt * 128, 128), ones_col[:],
                        start=(kt == 0), stop=(kt == NT - 1))),
                        reads=AT_bufs(pp, kt) + ["ones_col"], writes=["P7_0"],
                        tag=tA + f"dn{qb}")
                fw.op("vector", (lambda jt=jt: v.reciprocal(
                    recipd[:, jt:jt + 1], PB7[:, jt:jt + 1])),
                    reads=["P7_0"], writes=["recipd"], tag=tA + "dnr")
                sri = jglob % 3
                gi = jglob % 3
                xri = jglob % 2
                fw.dma("sr", (lambda b=b, jglob=jglob, sri=sri: sy.dma_start(
                    dpair_s(sri), d_ssp[b * N + jglob * 128: b * N + jglob * 128 + 128, :])),
                    reads=[f"hs{b}_{jglob}"], writes=sr_bufs(sri), tag=tA + "srl")
                fw.dma("sr", (lambda b=b, jglob=jglob, gi=gi: sy.dma_start(
                    dpair_g(gi), d_gsp[b * N + jglob * 128: b * N + jglob * 128 + 128, :])),
                    reads=[f"hg{b}_{jglob}"], writes=gr_bufs(gi), tag=tA + "grl")
                fw.dma("xt", (lambda b=b, jglob=jglob, xri=xri: sy.dma_start(
                    dpair_x(xri), d_x_tok[b * N + jglob * 128: b * N + jglob * 128 + 128, :])),
                    writes=xr_bufs(xri), tag=tA + "xr2")
                oi = jglob % 2
                for dh in range(2):
                    pb = dh % 2
                    for kt in range(NT):
                        fw.op("tensor", (lambda kt=kt, jt=jt, dh=dh, pb=pb, pp=pp: t.matmul(
                            PB[pb][:], AT_ap(pp, kt, jt * 128, 128),
                            sbf_ap(kt, dh * 512, 512),
                            start=(kt == 0), stop=(kt == NT - 1))),
                            reads=AT_bufs(pp, kt) + sbf_bufs(kt), writes=[psb[pb]],
                            tag=tA + f"pv{qb}")
                    fw.op("vector", (lambda jt=jt, dh=dh, pb=pb, sri=sri: v.scalar_tensor_tensor(
                        MSC[:, 0:512], PB[pb][:], recipd[:, jt:jt + 1],
                        dpair_s(sri)[:, dh * 512:(dh + 1) * 512], ALU.mult, ALU.add)),
                        reads=[psb[pb], "recipd", sr_bufs(sri)[dh]],
                        writes=["D22a", "D22b"], tag=tA + f"cv{qb}")
                    fw.op("vector", (lambda dh=dh: v.tensor_add(
                        MSC[:, 0:512], MSC[:, 0:512], GB[:, dh * 512:(dh + 1) * 512])),
                        reads=["D22a", "D22b"] + [GB_bufs[dh]], writes=["D22a", "D22b"])
                    fw.op("scalar", (lambda dh=dh, oi=oi: sc.activation(
                        dpair_o(oi)[:, dh * 512:(dh + 1) * 512], MSC[:, 0:512], AF.Tanh,
                        scale=betacol[:, 0:1])),
                        reads=["D22a", "D22b", "betacol"], writes=[or_bufs(oi)[dh]],
                        tag=tA + "tanh")
                    fw.op("gpsimd", (lambda dh=dh, oi=oi, gi=gi: gp.tensor_mul(
                        dpair_o(oi)[:, dh * 512:(dh + 1) * 512],
                        dpair_o(oi)[:, dh * 512:(dh + 1) * 512],
                        dpair_g(gi)[:, dh * 512:(dh + 1) * 512])),
                        reads=[or_bufs(oi)[dh], gr_bufs(gi)[dh]], writes=[or_bufs(oi)[dh]])
                    fw.op("gpsimd", (lambda dh=dh, oi=oi, xri=xri: gp.tensor_add(
                        dpair_o(oi)[:, dh * 512:(dh + 1) * 512],
                        dpair_o(oi)[:, dh * 512:(dh + 1) * 512],
                        dpair_x(xri)[:, dh * 512:(dh + 1) * 512])),
                        reads=[or_bufs(oi)[dh], xr_bufs(xri)[dh]], writes=[or_bufs(oi)[dh]])
                fw.dma("gout", (lambda b=b, jglob=jglob, oi=oi: sc.dma_start(
                    d_out[b * N + jglob * 128: b * N + jglob * 128 + 128, :], dpair_o(oi))),
                    reads=or_bufs(oi), writes=[f"ho{b}_{jglob}"], tag=tA + "out")

    for b in range(NB):
        emit_batch(b)

    fw.op("sync", lambda: None,
          reads=[f"ho{b}_{j}" for b in range(NB) for j in range(NT)], tag="final")
    fw.ops[-1]["final"] = True

    with nc.Block() as block:
        fw.emit(block, sems)

    nc._ctx_keepalive = ctx
    return nc


_NC_CACHE = None


def kernel(L0_attn_x, L0_attn_W_spin, L0_attn_b_spin, L0_attn_W_global, L0_attn_W_J,
           L0_attn_beta, L0_ffn_centroids, L0_ffn_expert_biases, L0_ffn_Wr, L0_ffn_br,
           L0_ffn_Ws, L0_ffn_bs):
    global _NC_CACHE
    x = np.asarray(L0_attn_x, dtype=np.float32)

    WrT = np.ascontiguousarray(np.asarray(L0_ffn_Wr, np.float32).transpose(0, 2, 1)).reshape(NE * D, D)
    WsT = np.ascontiguousarray(np.asarray(L0_ffn_Ws, np.float32).transpose(0, 2, 1)).reshape(2 * D, D)
    WspinT = np.ascontiguousarray(np.asarray(L0_attn_W_spin, np.float32).T)
    WJT = np.ascontiguousarray(np.asarray(L0_attn_W_J, np.float32).T)
    WglobT = np.ascontiguousarray(np.asarray(L0_attn_W_global, np.float32).T)
    centT = np.ascontiguousarray(np.asarray(L0_ffn_centroids, np.float32).T)
    br = np.asarray(L0_ffn_br, np.float32)
    bspinB = np.ascontiguousarray(np.broadcast_to(
        np.asarray(L0_attn_b_spin, np.float32)[None, :], (128, D)))
    bs = np.asarray(L0_ffn_bs, np.float32)
    bsB2 = np.ascontiguousarray(np.broadcast_to(bs[:, None, :], (2, 128, D))).reshape(256, D)
    ebB = np.ascontiguousarray(np.broadcast_to(
        np.asarray(L0_ffn_expert_biases, np.float32)[None, :], (128, NE)))
    betacol = np.ascontiguousarray(np.broadcast_to(
        np.asarray(L0_attn_beta, np.float32).reshape(1, 1), (128, 1)))
    identbf = np.eye(128, dtype=ml_dtypes.bfloat16)
    ones_col = np.ones((128, 1), dtype=ml_dtypes.bfloat16)
    ones_row = np.ones((1, 128), dtype=ml_dtypes.bfloat16)
    ones_rowf = np.ones((1, 128), dtype=np.float32)

    if _NC_CACHE is None:
        _NC_CACHE = build_nc()
    nc = _NC_CACHE

    in_maps = []
    for cidx in range(NCORES):
        xs = x[cidx * NB:(cidx + 1) * NB]
        x_tok = np.ascontiguousarray(xs.reshape(NB * N, D))
        x_dm = np.ascontiguousarray(xs.transpose(0, 2, 1).reshape(NB * D, N))
        in_maps.append({
            "x_tok": x_tok, "x_dm": x_dm, "WrT": WrT, "WsT": WsT,
            "WspinT": WspinT, "WJT": WJT, "WglobT": WglobT, "centT": centT,
            "br": br, "bspinB": bspinB, "bsB2": bsB2, "ebB": ebB,
            "betacol": betacol, "identbf": identbf, "ones_col": ones_col,
            "ones_row": ones_row, "ones_rowf": ones_rowf,
        })

    res = run_bass_kernel_spmd(nc, in_maps, core_ids=list(range(NCORES)))
    outs = [np.asarray(r["out"]).reshape(NB, N, D) for r in res.results]
    return np.concatenate(outs, axis=0)


# version 18 (final)
# speedup vs baseline: 1.0977x; 1.0288x over previous
import numpy as np
import ml_dtypes
import concourse.bass as bass
import concourse.mybir as mybir
from concourse.bass_utils import run_bass_kernel_spmd

F32 = mybir.dt.float32
BF16 = mybir.dt.bfloat16
AF = mybir.ActivationFunctionType
ALU = mybir.AluOpType
AX = mybir.AxisListType

NCORES = 8
NB = 4
N = 2048
D = 1024
NCH = 8
NT = 16
NE = 16
TOPK = 2
EPS = 1e-5
SCL = 1.0 / 32.0

ENGS = ("tensor", "scalar", "vector", "gpsimd", "sync")


class Fw:

    def __init__(self):
        self.ops = []
        self.streams = {}

    def op(self, eng, fn, reads=(), writes=(), tag=""):
        self.ops.append(dict(eng=eng, fn=fn, reads=tuple(reads), writes=tuple(writes),
                             dma=None, tag=tag, inc=False, deps=None))

    def dma_stream(self, name, queue, nslots, sem_alloc):
        self.streams[name] = dict(queue=queue, sems=[sem_alloc() for _ in range(nslots)],
                                  uses=[0] * nslots, n=0, nslots=nslots)

    def dma(self, stream, fn, reads=(), writes=(), tag=""):
        st = self.streams[stream]
        slot = st["n"] % st["nslots"]
        prior = st["uses"][slot]
        st["uses"][slot] = prior + 1
        st["n"] += 1
        self.ops.append(dict(eng=st["queue"], fn=fn, reads=tuple(reads), writes=tuple(writes),
                             dma=(stream, slot, prior), tag=tag, inc=False, deps=None))

    def analyze(self):
        last_write = {}
        readers = {}
        for i, o in enumerate(self.ops):
            deps = set()
            for b in o["reads"]:
                w = last_write.get(b)
                if w is not None:
                    deps.add(w)
            for b in o["writes"]:
                w = last_write.get(b)
                if w is not None:
                    deps.add(w)
                for r in readers.get(b, ()):
                    deps.add(r)
            deps.discard(i)
            fd = []
            for j in deps:
                oj = self.ops[j]
                if (oj["dma"] is None and o["dma"] is None
                        and oj["eng"] == "tensor" and o["eng"] == "tensor"):
                    continue
                fd.append(j)
            o["deps"] = sorted(fd)
            for j in o["deps"]:
                if self.ops[j]["dma"] is None:
                    self.ops[j]["inc"] = True
            for b in o["reads"]:
                readers.setdefault(b, []).append(i)
            for b in o["writes"]:
                last_write[b] = i
                readers[b] = [i]
        cnt = {e: 0 for e in ENGS}
        for o in self.ops:
            if o["dma"] is None:
                if o["inc"]:
                    cnt[o["eng"]] += 1
                o["cnt"] = cnt[o["eng"]]

    def emit(self, block, eng_sems):
        self.analyze()
        guar = {e: {} for e in ENGS}

        def signal(j):
            oj = self.ops[j]
            if oj["dma"] is None:
                return ("E", oj["eng"]), oj["cnt"]
            stream, slot, prior = oj["dma"]
            return ("D", stream, slot), 16 * (prior + 1)

        def sem_handle(key):
            if key[0] == "E":
                return eng_sems[key[1]]
            return self.streams[key[1]]["sems"][key[2]]

        def run_engine(eng_name, eng):
            g = guar[eng_name]
            for o in self.ops:
                if o["eng"] != eng_name:
                    continue
                if o.get("final"):
                    for stname, st in sorted(self.streams.items()):
                        for slot in range(st["nslots"]):
                            if st["uses"][slot] == 0:
                                continue
                            k = ("D", stname, slot)
                            vv = 16 * st["uses"][slot]
                            if g.get(k, 0) < vv:
                                eng.wait_ge(st["sems"][slot], vv)
                                g[k] = vv
                waits = {}
                for j in o["deps"]:
                    k, vv = signal(j)
                    if waits.get(k, 0) < vv:
                        waits[k] = vv
                if o["dma"] is not None:
                    stream, slot, prior = o["dma"]
                    if prior > 0:
                        k = ("D", stream, slot)
                        if waits.get(k, 0) < 16 * prior:
                            waits[k] = 16 * prior
                for k, vv in sorted(waits.items(), key=lambda kv: str(kv[0])):
                    if g.get(k, 0) >= vv:
                        continue
                    eng.wait_ge(sem_handle(k), vv)
                    g[k] = vv
                instr = o["fn"]()
                if o["dma"] is not None:
                    stream, slot, prior = o["dma"]
                    instr.then_inc(self.streams[stream]["sems"][slot], 16)
                elif o["inc"]:
                    instr.then_inc(eng_sems[o["eng"]], 1)

        for name in ENGS:
            dec = getattr(block, name)

            def make(nm):
                def body(e):
                    run_engine(nm, e)
                return body
            dec(make(name))


def build_nc():
    nc = bass.Bass("TRN2", target_bir_lowering=False)
    t, sc, v, gp, sy = nc.tensor, nc.scalar, nc.vector, nc.gpsimd, nc.sync

    d_x_tok = nc.dram_tensor("x_tok", [NB * N, D], F32, kind="ExternalInput").ap()
    d_x_dm = nc.dram_tensor("x_dm", [NB * D, N], F32, kind="ExternalInput").ap()
    d_WrT = nc.dram_tensor("WrT", [NE * D, D], F32, kind="ExternalInput").ap()
    d_WsT = nc.dram_tensor("WsT", [2 * D, D], F32, kind="ExternalInput").ap()
    d_WspinT = nc.dram_tensor("WspinT", [D, D], F32, kind="ExternalInput").ap()
    d_WJT = nc.dram_tensor("WJT", [D, D], F32, kind="ExternalInput").ap()
    d_WglobT = nc.dram_tensor("WglobT", [D, D], F32, kind="ExternalInput").ap()
    d_centT = nc.dram_tensor("centT", [D, NE], F32, kind="ExternalInput").ap()
    d_br = nc.dram_tensor("br", [NE, D], F32, kind="ExternalInput").ap()
    d_bspinB = nc.dram_tensor("bspinB", [128, D], F32, kind="ExternalInput").ap()
    d_bsB2 = nc.dram_tensor("bsB2", [2 * 128, D], F32, kind="ExternalInput").ap()
    d_ebB = nc.dram_tensor("ebB", [128, NE], F32, kind="ExternalInput").ap()
    d_betacol = nc.dram_tensor("betacol", [128, 1], F32, kind="ExternalInput").ap()
    d_ident = nc.dram_tensor("identbf", [128, 128], BF16, kind="ExternalInput").ap()
    d_ones_col = nc.dram_tensor("ones_col", [128, 1], BF16, kind="ExternalInput").ap()
    d_ones_row = nc.dram_tensor("ones_row", [1, 128], BF16, kind="ExternalInput").ap()
    d_ones_rowf = nc.dram_tensor("ones_rowf", [1, 128], F32, kind="ExternalInput").ap()

    d_out = nc.dram_tensor("out", [NB * N, D], F32, kind="ExternalOutput").ap()
    d_gsp = nc.dram_tensor("g_spill", [NB * N, D], F32, kind="Internal").ap()
    d_wc = nc.dram_tensor("w_cache", [6 * 128, 4 * 1024], BF16, kind="Internal").ap()
    d_ssp = nc.dram_tensor("s_spill", [NB * N, D], F32, kind="Internal").ap()

    import contextlib
    ctx = contextlib.ExitStack()

    def sb(name, shape, dt):
        return ctx.enter_context(nc.sbuf_tensor("sb_" + name, shape, dt))

    def psa(name, shape, dt):
        return ctx.enter_context(nc.psum_tensor(name, shape, dt))

    A = sb("A", [128, 16 * 1024], BF16)
    Bt = sb("Bt", [128, 16 * 1024], BF16)
    Ct = sb("Ct", [128, 16 * 1024], BF16)
    Dt = sb("Dt", [128, 24 * 512], F32)
    W = [sb(f"W{i}", [128, 4 * 1024], BF16) for i in range(4)]
    J = sb("J", [128, 8 * 512], BF16)

    bspinB = sb("bspinB", [128, D], F32)
    bsB = sb("bsB", [128, D], F32)
    ebB = sb("ebB", [128, NE], F32)
    betacol = sb("betacol", [128, 1], F32)
    identbf = sb("identbf", [128, 128], BF16)
    ones_col = sb("ones_col", [128, 1], BF16)
    ones_row = sb("ones_row", [1, 128], BF16)
    ones_rowf = sb("ones_rowf", [1, 128], F32)
    centT = sb("centT", [128, NCH * NE], BF16)
    br_bf = sb("br_bf", [16, D], BF16)
    gT_bf = sb("gT_bf", [16, N], BF16)
    m_sum = sb("m_sum", [128, NT], F32)
    q_sum = sb("q_sum", [128, NT], F32)
    m_col = sb("m_col", [128, NT], F32)
    r_col = sb("r_col", [128, NT], F32)
    rm_col = sb("rm_col", [128, NT], F32)
    halfr = sb("halfr", [128, NT], F32)
    st1 = sb("st1", [128, NT], F32)
    st2 = sb("st2", [128, NT], F32)
    g_r = sb("g_r", [128, NT * NE], F32)
    gr_r = sb("gr_r", [128, NT * NE], F32)
    g_bf = sb("g_bf", [128, NT * NE], BF16)
    top8 = sb("top8", [128, 8], F32)
    pmneg = sb("pmneg", [128, 1], F32)
    precip = sb("precip", [128, 1], F32)
    snorm2 = sb("snorm2", [128, NT], F32)
    smax2c = sb("smax2c", [128, 1], F32)
    smax2bf = sb("smax2bf", [128, 1], BF16)
    cval = sb("cval", [1, 2], F32)
    cneg = sb("cneg", [128, 1], F32)
    recipd = sb("recipd", [128, 4], F32)
    mf_bf = sb("mf_bf", [128, NCH], BF16)
    mfs = sb("mfs", [128, 1], F32)

    PB = [psa(f"PB{i}", [128, 512], F32) for i in range(6)]
    PBb = psa("PBb", [128, 1024], BF16)
    PB7 = psa("PB7", [128, 512], F32)
    psb = {0: "PB0", 1: "PB1", 2: "PB2", 3: "PB3", 4: "PB4", 5: "PB5", 7: "PB7"}
    P7ALL = [f"P7_{u}" for u in range(32)]

    fw = Fw()
    sems = {e: ctx.enter_context(nc.semaphore(f"es_{e}")) for e in ENGS}

    _semn = [0]

    def mksem():
        _semn[0] += 1
        return ctx.enter_context(nc.semaphore(f"dsem{_semn[0]}"))

    fw.dma_stream("xcl", "gpsimd", 2, mksem)
    fw.dma_stream("wl", "gpsimd", 4, mksem)
    fw.dma_stream("cl", "gpsimd", 2, mksem)
    fw.dma_stream("xt", "sync", 3, mksem)
    fw.dma_stream("sw", "scalar", 4, mksem)
    fw.dma_stream("sr", "sync", 4, mksem)
    fw.dma_stream("gout", "scalar", 4, mksem)
    fw.dma_stream("cst", "sync", 2, mksem)
    fw.dma_stream("wc", "scalar", 2, mksem)
    fw.dma_stream("wls", "sync", 2, mksem)
    fw.dma_stream("wlc", "scalar", 2, mksem)

    def aid(lo, hi):
        return [f"A{u}" for u in range(lo, hi)]

    def did(u):
        return f"D{u}"

    def wid(slot):
        return f"W{slot}"

    def xcT(c, col0, w):
        return A[:, c * 2048 + col0: c * 2048 + col0 + w]

    def xcT_bufs(c, col0, w):
        lo = (c * 2048 + col0) // 1024
        hi = (c * 2048 + col0 + w + 1023) // 1024
        return aid(lo, hi)

    def AT_ap(pp, kt, q0, w):
        base = kt * 512 + q0
        return A[:, base:base + w]

    def AT_bufs(pp, kt):
        u = (kt * 512) // 1024
        return aid(u, u + 1)

    def sbf_ap(j, col0, w):
        return Bt[:, j * 1024 + col0: j * 1024 + col0 + w]

    def sbf_bufs(j):
        return [f"B{j}"]

    def sT_ap(c, col0, w):
        return Ct[:, c * 2048 + col0: c * 2048 + col0 + w]

    def sT_bufs(c):
        return [f"C{2 * c}", f"C{2 * c + 1}"]

    def routed_ap(j, col0, w):
        tens = Bt if j < 8 else Ct
        jj = j if j < 8 else j - 8
        return tens[:, jj * 2048: jj * 2048 + 2048].bitcast(F32)[:, col0:col0 + w]

    def routed_bufs(j):
        if j < 8:
            return [f"B{2 * j}", f"B{2 * j + 1}"]
        return [f"C{2 * (j - 8)}", f"C{2 * (j - 8) + 1}"]

    def eT_ap(c, w=512):
        return J[:, c * 512: c * 512 + w]

    def eT_bufs(c):
        return [f"J{c}"]

    def dpair_x(i):
        return Dt[:, (2 * i) * 512:(2 * i + 2) * 512]

    def xr_bufs(i):
        return [did(2 * i), did(2 * i + 1)]

    def dpair_g(i):
        return Dt[:, (4 + 2 * i) * 512:(6 + 2 * i) * 512]

    def gr_bufs(i):
        return [did(4 + 2 * i), did(5 + 2 * i)]

    def dpair_s(i):
        return Dt[:, (10 + 2 * i) * 512:(12 + 2 * i) * 512]

    def sr_bufs(i):
        return [did(10 + 2 * i), did(11 + 2 * i)]

    def dpair_o(i):
        return Dt[:, (16 + 2 * i) * 512:(18 + 2 * i) * 512]

    def or_bufs(i):
        return [did(16 + 2 * i), did(17 + 2 * i)]

    GB = Dt[:, 20 * 512:22 * 512]
    GB_bufs = [did(20), did(21)]
    MSC = Dt[:, 22 * 512:23 * 512]
    MSC_bufs = [did(22)]
    mrow_bf_ap = Dt[0:1, 23 * 512:23 * 512 + 256].bitcast(BF16)
    ROW_bufs = [did(23)]
    sm_row = Dt[0:1, 23 * 512 + 256:23 * 512 + 384]
    grow = Dt[0:1, 20 * 512:20 * 512 + 1024]
    GROW_bufs = [did(20), did(21)]

    fw.dma("cst", lambda: sy.dma_start(bspinB[:], d_bspinB), writes=["bspinB"])
    fw.dma("cst", lambda: sy.dma_start(ebB[:], d_ebB), writes=["ebB"])
    fw.dma("cst", lambda: sy.dma_start(betacol[:], d_betacol), writes=["betacol"])
    fw.dma("cst", lambda: sy.dma_start(identbf[:], d_ident), writes=["identbf"])
    fw.dma("cst", lambda: sy.dma_start(ones_col[:], d_ones_col), writes=["ones_col"])
    fw.dma("cst", lambda: sy.dma_start(ones_row[:], d_ones_row), writes=["ones_row"])
    fw.dma("cst", lambda: sy.dma_start(ones_rowf[:], d_ones_rowf), writes=["ones_rowf"])
    fw.dma("cst", lambda: sy.dma_start(dpair_x(0), d_bsB2[0:128, :]), writes=xr_bufs(0))
    fw.dma("cst", lambda: sy.dma_start(dpair_g(0), d_bsB2[128:256, :]), writes=gr_bufs(0))
    fw.op("vector", lambda: v.tensor_add(bsB[:], dpair_x(0), dpair_g(0)),
          reads=xr_bufs(0) + gr_bufs(0), writes=["bsB"])
    fw.op("vector", lambda: v.tensor_scalar_mul(bsB[:], bsB[:], 0.5),
          reads=["bsB"], writes=["bsB"])
    fw.dma("cl", lambda: gp.dma_start(
        centT[:].rearrange("p (c e) -> p c e", e=NE),
        d_centT.rearrange("(c p) e -> p c e", p=128)), writes=["centT"])
    fw.dma("cl", lambda: gp.dma_start(br_bf[:], d_br), writes=["br_bf"])

    def load_w_half(slot, dram, row0, tag):
        def fn(sl=slot, dr=dram, r0=row0):
            return gp.dma_start(
                W[sl][:].rearrange("p (c o) -> p c o", o=D),
                dr[r0:r0 + 512, :].rearrange("(c p) o -> p c o", p=128))
        fw.dma("wl", fn, writes=[wid(slot)], tag=tag)

    def load_w_cached(slot, dram, row0, cidx, first, stream, tag):
        if first:
            load_w_half(slot, dram, row0, tag)
            eng = sc if fw.streams["wc"]["queue"] == "scalar" else sy
            fw.dma("wc", (lambda sl=slot, ci=cidx, eng=eng: eng.dma_start(
                d_wc[ci * 128:(ci + 1) * 128, :], W[sl][:])),
                reads=[wid(slot)], writes=[f"wc{cidx}"], tag=tag + "st")
        else:
            qn = fw.streams[stream]["queue"]
            eng = {"scalar": sc, "sync": sy, "gpsimd": gp}[qn]
            fw.dma(stream, (lambda sl=slot, ci=cidx, eng=eng: eng.dma_start(
                W[sl][:], d_wc[ci * 128:(ci + 1) * 128, :])),
                reads=[f"wc{cidx}"], writes=[wid(slot)], tag=tag)

    def w_rhs(pair, c, oh):
        sl = pair[c // 4]
        base = (c % 4) * 1024 + oh * 512
        return W[sl][:, base:base + 512]

    def w_sl(pair, c, col0, w):
        sl = pair[c // 4]
        base = (c % 4) * 1024 + col0
        return W[sl][:, base:base + w]

    wl_pairs = [(0, 1), (2, 3)]

    def emit_xcl_half(b, half):
        fw.dma("xcl", (lambda b=b, half=half: gp.dma_start(
            A[:, half * 8192:(half + 1) * 8192].rearrange("p (c n) -> p c n", n=N),
            d_x_dm[b * D + half * 512: b * D + (half + 1) * 512, :]
            .rearrange("(c p) n -> p c n", p=128))),
            writes=aid(8 * half, 8 * half + 8), tag=f"b{b}_xcl{half}")

    def emit_batch(b):
        tA = f"b{b}_"

        emit_xcl_half(b, 0)
        emit_xcl_half(b, 1)

        for c in range(NCH):
            fw.op("scalar", (lambda c=c: sc.activation(
                J[:, 0:2048], xcT(c, 0, 2048), AF.Square)),
                reads=xcT_bufs(c, 0, 2048), writes=["J0", "J1", "J2", "J3"],
                tag=tA + "sq")
            for j in range(NT):
                fw.op("tensor", (lambda c=c, j=j: t.matmul(
                    PB[2][:, j:j + 1], xcT(c, j * 128, 128), ones_col[:],
                    start=True, stop=True)),
                    reads=xcT_bufs(c, j * 128, 128) + ["ones_col"], writes=["PB2"],
                    tag=tA + "mc")
                fw.op("tensor", (lambda c=c, j=j: t.matmul(
                    PB[3][:, j:j + 1], J[:, j * 128:(j + 1) * 128], ones_col[:],
                    start=True, stop=True)),
                    reads=["J0", "J1", "J2", "J3", "ones_col"], writes=["PB3"],
                    tag=tA + "qc")
            if c == 0:
                fw.op("vector", (lambda: v.tensor_copy(m_sum[:], PB[2][:, 0:NT])),
                      reads=["PB2"], writes=["m_sum"], tag=tA + "mca")
                fw.op("vector", (lambda: v.tensor_copy(q_sum[:], PB[3][:, 0:NT])),
                      reads=["PB3"], writes=["q_sum"], tag=tA + "qca")
            else:
                fw.op("vector", (lambda: v.tensor_add(m_sum[:], m_sum[:], PB[2][:, 0:NT])),
                      reads=["m_sum", "PB2"], writes=["m_sum"], tag=tA + "mca")
                fw.op("vector", (lambda: v.tensor_add(q_sum[:], q_sum[:], PB[3][:, 0:NT])),
                      reads=["q_sum", "PB3"], writes=["q_sum"], tag=tA + "qca")

        for quad in range(4):
            for jj in range(4):
                j = quad * 4 + jj
                for c in range(NCH):
                    fw.op("tensor", (lambda c=c, j=j, jj=jj: t.matmul(
                        PB[3][0:1, jj * 128:jj * 128 + 128], ones_col[:],
                        xcT(c, j * 128, 128), start=(c == 0), stop=(c == NCH - 1))),
                        reads=["ones_col"] + xcT_bufs(c, j * 128, 128), writes=["PB3"],
                        tag=tA + "mrow")
            fw.op("scalar", (lambda: sc.activation(mrow_bf_ap, PB[3][0:1, :], AF.Copy,
                                                   scale=1.0 / D)),
                  reads=["PB3"], writes=ROW_bufs, tag=tA + "mrowe")
            fw.op("tensor", (lambda: t.matmul(PB[2][:], ones_row[:], mrow_bf_ap,
                                              start=True, stop=True)),
                  reads=["ones_row"] + ROW_bufs, writes=["PB2"], tag=tA + "mB")
            for c in range(NCH):
                fw.op("vector", (lambda c=c, quad=quad: v.tensor_sub(
                    xcT(c, quad * 512, 512), xcT(c, quad * 512, 512), PB[2][:])),
                    reads=xcT_bufs(c, quad * 512, 512) + ["PB2"],
                    writes=xcT_bufs(c, quad * 512, 512), tag=tA + "ctr")

        fw.op("vector", lambda: v.tensor_scalar_mul(m_col[:], m_sum[:], 1.0 / D),
              reads=["m_sum"], writes=["m_col"])
        fw.op("vector", lambda: v.tensor_scalar_mul(st1[:], q_sum[:], 1.0 / D),
              reads=["q_sum"], writes=["st1"])
        fw.op("vector", lambda: v.tensor_mul(st2[:], m_col[:], m_col[:]),
              reads=["m_col"], writes=["st2"])
        fw.op("vector", lambda: v.tensor_sub(st1[:], st1[:], st2[:]),
              reads=["st1", "st2"], writes=["st1"])
        fw.op("vector", lambda: v.tensor_scalar_add(st1[:], st1[:], EPS),
              reads=["st1"], writes=["st1"])
        fw.op("scalar", lambda: sc.activation(st2[:], st1[:], AF.Sqrt),
              reads=["st1"], writes=["st2"])
        fw.op("vector", lambda: v.reciprocal(r_col[:], st2[:]),
              reads=["st2"], writes=["r_col"])
        fw.op("vector", lambda: v.tensor_mul(st2[:], r_col[:], r_col[:]),
              reads=["r_col"], writes=["st2"])
        fw.op("vector", lambda: v.tensor_mul(st2[:], st2[:], st1[:]),
              reads=["st2", "st1"], writes=["st2"])
        fw.op("vector", lambda: v.tensor_scalar(st2[:], st2[:], -0.5, 1.5, ALU.mult, ALU.add),
              reads=["st2"], writes=["st2"])
        fw.op("vector", lambda: v.tensor_mul(r_col[:], r_col[:], st2[:]),
              reads=["r_col", "st2"], writes=["r_col"])
        fw.op("vector", lambda: v.tensor_mul(rm_col[:], r_col[:], m_col[:]),
              reads=["r_col", "m_col"], writes=["rm_col"])
        fw.op("vector", lambda: v.tensor_scalar_mul(halfr[:], r_col[:], 0.5),
              reads=["r_col"], writes=["halfr"])

        for j in range(NT):
            for c in range(NCH):
                fw.op("tensor", (lambda c=c, j=j: t.matmul(
                    PB7[:, j * NE:(j + 1) * NE], xcT(c, j * 128, 128), centT[:, c * NE:(c + 1) * NE],
                    start=(c == 0), stop=(c == NCH - 1))),
                    reads=xcT_bufs(c, j * 128, 128) + ["centT"], writes=[f"P7_{j}"],
                    tag=tA + "rt")
        for j in range(NT):
            fw.op("vector", (lambda j=j: v.tensor_scalar(
                g_r[:, j * NE:(j + 1) * NE], PB7[:, j * NE:(j + 1) * NE], r_col[:, j:j + 1], None, ALU.mult)),
                reads=[f"P7_{j}", "r_col"], writes=["g_r"], tag=tA + "rte")
            fw.op("vector", (lambda j=j: v.tensor_add(
                g_r[:, j * NE:(j + 1) * NE], g_r[:, j * NE:(j + 1) * NE], ebB[:])),
                reads=["g_r", "ebB"], writes=["g_r"])
            fw.op("vector", (lambda j=j: v.reduce_max(
                pmneg[:], g_r[:, j * NE:(j + 1) * NE], axis=AX.X)),
                reads=["g_r"], writes=["pmneg"])
            fw.op("vector", lambda: v.tensor_scalar_mul(pmneg[:], pmneg[:], -1.0),
                  reads=["pmneg"], writes=["pmneg"])
            fw.op("scalar", (lambda j=j: sc.activation(
                g_r[:, j * NE:(j + 1) * NE], g_r[:, j * NE:(j + 1) * NE], AF.Exp,
                bias=pmneg[:, 0:1], accum_out=precip[:, 0:1])),
                reads=["g_r", "pmneg"], writes=["g_r", "precip"], tag=tA + "rtx")
            fw.op("vector", lambda: v.reciprocal(precip[:], precip[:]),
                  reads=["precip"], writes=["precip"])
            fw.op("vector", (lambda j=j: v.tensor_scalar(
                g_r[:, j * NE:(j + 1) * NE], g_r[:, j * NE:(j + 1) * NE],
                precip[:, 0:1], None, ALU.mult)),
                reads=["g_r", "precip"], writes=["g_r"])
            fw.op("vector", (lambda j=j: v.max(
                out=top8[:], in_=g_r[:, j * NE:(j + 1) * NE])),
                reads=["g_r"], writes=["top8"])
            fw.op("vector", (lambda j=j: v.tensor_scalar(
                st1[:, 0:NE], g_r[:, j * NE:(j + 1) * NE], top8[:, TOPK - 1:TOPK],
                None, ALU.is_ge)),
                reads=["g_r", "top8"], writes=["st1"])
            fw.op("vector", (lambda j=j: v.tensor_mul(
                g_r[:, j * NE:(j + 1) * NE], g_r[:, j * NE:(j + 1) * NE], st1[:, 0:NE])),
                reads=["g_r", "st1"], writes=["g_r"])
            fw.op("vector", (lambda j=j: v.tensor_scalar(
                gr_r[:, j * NE:(j + 1) * NE], g_r[:, j * NE:(j + 1) * NE],
                r_col[:, j:j + 1], None, ALU.mult)),
                reads=["g_r", "r_col"], writes=["gr_r"])
            fw.op("vector", (lambda j=j: v.tensor_copy(
                g_bf[:, j * NE:(j + 1) * NE], g_r[:, j * NE:(j + 1) * NE])),
                reads=["g_r"], writes=[f"gbf{j}"])

        def emit_gT_br(j):
            fw.op("tensor", (lambda j=j: t.transpose(
                PBb[0:NE, 0:128], g_bf[:, j * NE:(j + 1) * NE], identbf[:])),
                reads=[f"gbf{j}", "identbf"], writes=["PBb"], tag=tA + "gT")
            fw.op("vector", (lambda j=j: v.tensor_copy(
                gT_bf[0:16, j * 128:(j + 1) * 128], PBb[0:NE, 0:128])),
                reads=["PBb"], writes=[f"gTb{j}"])
            for oh in range(2):
                pbk = 4 + (j * 2 + oh) % 2
                pbap = PB[4] if pbk == 4 else PB[5]
                fw.op("tensor", (lambda j=j, oh=oh, pbap=pbap: t.matmul(
                    pbap[:], gT_bf[0:16, j * 128:(j + 1) * 128],
                    br_bf[0:16, oh * 512:(oh + 1) * 512], start=True, stop=True)),
                    reads=[f"gTb{j}", "br_bf"], writes=[psb[pbk]], tag=tA + "br")
                fw.op("vector", (lambda j=j, oh=oh, pbap=pbap, pbk=pbk: v.scalar_tensor_tensor(
                    routed_ap(j, oh * 512, 512), pbap[:], rm_col[:, j:j + 1],
                    bsB[:, oh * 512:(oh + 1) * 512], ALU.subtract, ALU.add)),
                    reads=[psb[pbk], "rm_col", "bsB"], writes=routed_bufs(j),
                    tag=tA + "brv")

        for e in range(NE + 2):
            pair = wl_pairs[e % 2]
            dram = d_WrT if e < NE else d_WsT
            row = e * D if e < NE else (e - NE) * D
            load_w_half(pair[0], dram, row, tA + f"w{e}a")
            load_w_half(pair[1], dram, row + 512, tA + f"w{e}b")
            for j in range(NT):
                pb0 = 2 * (j % 2)
                for c in range(NCH):
                    for oh in range(2):
                        fw.op("tensor", (lambda c=c, j=j, oh=oh, pb=pb0 + oh, pair=pair: t.matmul(
                            PB[pb][:], xcT(c, j * 128, 128), w_rhs(pair, c, oh),
                            start=(c == 0), stop=(c == NCH - 1))),
                            reads=xcT_bufs(c, j * 128, 128) + [wid(pair[c // 4])],
                            writes=[psb[pb0 + oh]], tag=tA + f"e{e}")
                if e == 0:
                    emit_gT_br(j)
                for oh in range(2):
                    pb = pb0 + oh
                    if e < NE:
                        scal = gr_r[:, j * NE + e:j * NE + e + 1]
                        srd = "gr_r"
                    else:
                        scal = halfr[:, j:j + 1]
                        srd = "halfr"
                    fw.op("vector", (lambda j=j, oh=oh, pb=pb, scal=scal: v.scalar_tensor_tensor(
                        routed_ap(j, oh * 512, 512), PB[pb][:], scal,
                        routed_ap(j, oh * 512, 512), ALU.mult, ALU.add)),
                        reads=[psb[pb], srd] + routed_bufs(j),
                        writes=routed_bufs(j), tag=tA + f"ev{e}")

        first = (b == 0)
        load_w_cached(0, d_WspinT, 0, 0, first, "wls", tA + "spA")
        load_w_cached(1, d_WspinT, 512, 1, first, "wls", tA + "spB")
        load_w_cached(2, d_WJT, 0, 2, first, "wlc", tA + "wjA")
        load_w_cached(3, d_WJT, 512, 3, first, "wlc", tA + "wjB")

        for j in range(NT):
            xr = j % 2
            gi = j % 3
            fw.dma("xt", (lambda b=b, j=j, xr=xr: sy.dma_start(
                dpair_x(xr), d_x_tok[b * N + j * 128: b * N + j * 128 + 128, :])),
                writes=xr_bufs(xr), tag=tA + "xr1")
            fw.op("vector", (lambda j=j, xr=xr, gi=gi: v.scalar_tensor_tensor(
                dpair_g(gi), dpair_x(xr), r_col[:, j:j + 1], routed_ap(j, 0, 1024),
                ALU.mult, ALU.add)),
                reads=xr_bufs(xr) + ["r_col"] + routed_bufs(j), writes=gr_bufs(gi),
                tag=tA + "xn")
            fw.op("scalar", (lambda gi=gi: sc.activation(
                dpair_g(gi), dpair_g(gi), AF.Silu)),
                reads=gr_bufs(gi), writes=gr_bufs(gi), tag=tA + "silu")
            fw.dma("sw", (lambda b=b, j=j, gi=gi: sc.dma_start(
                d_gsp[b * N + j * 128: b * N + j * 128 + 128, :], dpair_g(gi))),
                reads=gr_bufs(gi), writes=[f"hg{b}_{j}"], tag=tA + "gsp")

        spair = (0, 1)
        jpair = (2, 3)
        def emit_sT(j):
            for c in range(NCH):
                fw.op("tensor", (lambda j=j, c=c: t.transpose(
                    PBb[:, c * 128:(c + 1) * 128], sbf_ap(j, c * 128, 128), identbf[:])),
                    reads=sbf_bufs(j) + ["identbf"], writes=["PBb"], tag=tA + "sT")
            for c in range(NCH):
                fw.op("vector", (lambda j=j, c=c: v.tensor_copy(
                    sT_ap(c, j * 128, 128), PBb[:, c * 128:(c + 1) * 128])),
                    reads=["PBb"], writes=sT_bufs(c), tag=tA + "sTe")

        for j in range(NT):
            ss = j % 3
            pb0 = 2 * (j % 2)
            for c in range(NCH):
                for oh in range(2):
                    fw.op("tensor", (lambda c=c, j=j, oh=oh, pb=pb0 + oh: t.matmul(
                        PB[pb][:], xcT(c, j * 128, 128), w_rhs(spair, c, oh),
                        start=(c == 0), stop=(c == NCH - 1))),
                        reads=xcT_bufs(c, j * 128, 128) + [wid(spair[c // 4])],
                        writes=[psb[pb0 + oh]], tag=tA + "sm")
            for oh in range(2):
                pb = pb0 + oh
                fw.op("vector", (lambda j=j, oh=oh, pb=pb, ss=ss: v.scalar_tensor_tensor(
                    dpair_s(ss)[:, oh * 512:(oh + 1) * 512], PB[pb][:],
                    r_col[:, j:j + 1], bspinB[:, oh * 512:(oh + 1) * 512],
                    ALU.mult, ALU.add)),
                    reads=[psb[pb], "r_col", "bspinB"], writes=[sr_bufs(ss)[oh]],
                    tag=tA + "se")
            fw.op("scalar", (lambda j=j, ss=ss: sc.activation(
                sbf_ap(j, 0, 1024), dpair_s(ss), AF.Copy)),
                reads=sr_bufs(ss), writes=sbf_bufs(j), tag=tA + "sc")
            fw.op("vector", (lambda j=j: v.scalar_tensor_tensor(
                J[:, 2048:3072], sbf_ap(j, 0, 1024), 1.0, sbf_ap(j, 0, 1024),
                ALU.mult, ALU.mult, accum_out=snorm2[:, j:j + 1])),
                reads=sbf_bufs(j), writes=["J4", "J5", "snorm2"],
                tag=tA + "sn")
            fw.dma("sw", (lambda b=b, j=j, ss=ss: sc.dma_start(
                d_ssp[b * N + j * 128: b * N + j * 128 + 128, :], dpair_s(ss))),
                reads=sr_bufs(ss), writes=[f"hs{b}_{j}"], tag=tA + "ssp")
            if j > 0:
                emit_sT(j - 1)
        emit_sT(NT - 1)

        fw.op("vector", lambda: v.reduce_max(smax2c[:], snorm2[:], axis=AX.X),
              reads=["snorm2"], writes=["smax2c"], tag=tA + "sm2")
        fw.op("vector", lambda: v.tensor_copy(smax2bf[:], smax2c[:]),
              reads=["smax2c"], writes=["smax2bf"])
        fw.op("tensor", lambda: t.transpose(PBb[0:1, 0:128], smax2bf[:], identbf[:]),
              reads=["smax2bf", "identbf"], writes=["PBb"], tag=tA + "smT")
        fw.op("vector", lambda: v.tensor_copy(sm_row, PBb[0:1, 0:128]),
              reads=["PBb"], writes=ROW_bufs)
        fw.op("vector", lambda: v.reduce_max(cval[:, 0:1], sm_row, axis=AX.X),
              reads=ROW_bufs, writes=["cval"])

        load_w_cached(0, d_WglobT, 0, 4, first, "wls", tA + "wgA")
        load_w_cached(1, d_WglobT, 512, 5, first, "wls", tA + "wgB")

        def emit_mfglob():
            gpair = (0, 1)
            for oh in range(2):
                for kt in range(NT):
                    fw.op("tensor", (lambda kt=kt, oh=oh: t.matmul(
                        PB[3][0:1, 0:512], ones_col[:], sbf_ap(kt, oh * 512, 512),
                        start=(kt == 0), stop=(kt == NT - 1))),
                        reads=["ones_col"] + sbf_bufs(kt), writes=["PB3"],
                        tag=tA + "mfm")
                fw.op("scalar", (lambda oh=oh: sc.activation(
                    mrow_bf_ap, PB[3][0:1, 0:512], AF.Copy, scale=1.0 / N)),
                    reads=["PB3"], writes=ROW_bufs, tag=tA + "mfe")
                for c4 in range(4):
                    c = oh * 4 + c4
                    fw.op("tensor", (lambda c=c, c4=c4: t.transpose(
                        PBb[:, c4 * 128:c4 * 128 + 1],
                        mrow_bf_ap[0:1, c4 * 128:(c4 + 1) * 128], identbf[0:1, 0:1])),
                        reads=ROW_bufs + ["identbf"], writes=["PBb"], tag=tA + "mfT")
                    fw.op("vector", (lambda c=c, c4=c4: v.tensor_copy(
                        mf_bf[:, c:c + 1], PBb[:, c4 * 128:c4 * 128 + 1])),
                        reads=["PBb"], writes=["mf_bf"], tag=tA + "mfTe")
            for oh in range(2):
                for c in range(NCH):
                    fw.op("tensor", (lambda c=c, oh=oh: t.matmul(
                        PB[3][0:1, 0:512], mf_bf[:, c:c + 1], w_rhs(gpair, c, oh),
                        start=(c == 0), stop=(c == NCH - 1))),
                        reads=["mf_bf", wid(gpair[c // 4])], writes=["PB3"], tag=tA + "gl")
                fw.op("scalar", (lambda oh=oh: sc.activation(
                    grow[0:1, oh * 512:(oh + 1) * 512], PB[3][0:1, 0:512], AF.Copy)),
                    reads=["PB3"], writes=GROW_bufs, tag=tA + "gle")
            for oh in range(2):
                fw.op("tensor", (lambda oh=oh: t.matmul(
                    PB[3][:], ones_rowf[:], grow[0:1, oh * 512:(oh + 1) * 512],
                    start=True, stop=True)),
                    reads=["ones_rowf"] + GROW_bufs, writes=["PB3"], tag=tA + "gb")
                fw.op("vector", (lambda oh=oh: v.tensor_copy(
                    GB[:, oh * 512:(oh + 1) * 512], PB[3][:])),
                    reads=["PB3"], writes=[GB_bufs[oh]], tag=tA + "gbe")

        for qb in range(4):
            pp = qb % 2
            q0 = qb * 512
            def e2m_mm(c):
                half = c % 2
                fw.op("tensor", (lambda c=c, half=half: t.matmul(
                    PB[5][0:1, 0:512], ones_col[:],
                    MSC[:].bitcast(BF16)[:, half * 512:(half + 1) * 512],
                    start=(c == 0), stop=(c == NCH - 1))),
                    reads=["ones_col", f"D22{'ab'[half]}"], writes=["PB5"],
                    tag=tA + "e2m")

            for c in range(NCH):
                pbk = 4 if (c % 2 == 0) else 7
                pbap = PB[4] if pbk == 4 else PB7
                for c2 in range(NCH):
                    fw.op("tensor", (lambda c=c, c2=c2, pbap=pbap, q0=q0: t.matmul(
                        pbap[:], w_sl(jpair, c2, c * 128, 128), sT_ap(c2, q0, 512),
                        start=(c2 == 0), stop=(c2 == NCH - 1))),
                        reads=[wid(jpair[c2 // 4])] + sT_bufs(c2),
                        writes=(P7ALL if pbk == 7 else [psb[pbk]]), tag=tA + f"en{qb}")
                fw.op("scalar", (lambda c=c, pbap=pbap: sc.activation(
                    eT_ap(c), pbap[:], AF.Copy)),
                    reads=(P7ALL if pbk == 7 else [psb[pbk]]), writes=eT_bufs(c), tag=tA + f"ene{qb}")
                if c >= 2:
                    e2m_mm(c - 2)
                fw.op("scalar", (lambda c=c, half=c % 2: sc.activation(
                    MSC[:].bitcast(BF16)[:, half * 512:(half + 1) * 512],
                    eT_ap(c), AF.Square)),
                    reads=eT_bufs(c), writes=[f"D22{'ab'[c % 2]}"], tag=tA + "e2")
            e2m_mm(NCH - 2)
            e2m_mm(NCH - 1)
            def qk_group(kt):
                pb = 2 + (kt % 2)
                for c in range(NCH):
                    fw.op("tensor", (lambda c=c, kt=kt, pb=pb, q0=q0: t.matmul(
                        PB[pb][:], sT_ap(c, kt * 128, 128), eT_ap(c),
                        start=(c == 0), stop=(c == NCH - 1))),
                        reads=sT_bufs(c) + eT_bufs(c), writes=[psb[pb]],
                        tag=tA + f"qk{qb}")

            def exp_group(kt):
                pb = 2 + (kt % 2)
                fw.op("scalar", (lambda kt=kt, pb=pb, pp=pp: sc.activation(
                    AT_ap(pp, kt, 0, 512), PB[pb][:], AF.Exp, scale=SCL,
                    bias=cneg[:, 0:1])),
                    reads=[psb[pb], "cneg"], writes=AT_bufs(pp, kt), tag=tA + f"xp{qb}")

            qk_group(0)
            qk_group(1)
            fw.op("vector", lambda: v.reduce_max(cval[:, 1:2], PB[5][0:1, 0:512], axis=AX.X),
                  reads=["PB5"], writes=["cval"], tag=tA + "emx")
            fw.op("vector", lambda: v.tensor_add(cval[:, 1:2], cval[:, 1:2], cval[:, 0:1]),
                  reads=["cval"], writes=["cval"])
            fw.op("vector", lambda: v.tensor_scalar(
                cval[:, 1:2], cval[:, 1:2], -1.02 / 64.0, None, ALU.mult),
                reads=["cval"], writes=["cval"])
            fw.op("tensor", lambda: t.matmul(
                PB[5][:, 0:1], ones_rowf[:], cval[0:1, 1:2], start=True, stop=True),
                reads=["ones_rowf", "cval"], writes=["PB5"], tag=tA + "cb")
            fw.op("vector", lambda: v.tensor_copy(cneg[:], PB[5][:, 0:1]),
                  reads=["PB5"], writes=["cneg"])
            exp_group(0)
            exp_group(1)
            for kt in range(2, NT):
                qk_group(kt)
                exp_group(kt)
            if qb == 0:
                emit_mfglob()

            for jt in range(4):
                jglob = qb * 4 + jt
                for kt in range(NT):
                    fw.op("tensor", (lambda kt=kt, jt=jt, pp=pp: t.matmul(
                        PB7[:, jt:jt + 1], AT_ap(pp, kt, jt * 128, 128), ones_col[:],
                        start=(kt == 0), stop=(kt == NT - 1))),
                        reads=AT_bufs(pp, kt) + ["ones_col"], writes=["P7_0"],
                        tag=tA + f"dn{qb}")
                fw.op("vector", (lambda jt=jt: v.reciprocal(
                    recipd[:, jt:jt + 1], PB7[:, jt:jt + 1])),
                    reads=["P7_0"], writes=["recipd"], tag=tA + "dnr")
                sri = jglob % 3
                gi = jglob % 3
                xri = jglob % 2
                fw.dma("sr", (lambda b=b, jglob=jglob, sri=sri: sy.dma_start(
                    dpair_s(sri), d_ssp[b * N + jglob * 128: b * N + jglob * 128 + 128, :])),
                    reads=[f"hs{b}_{jglob}"], writes=sr_bufs(sri), tag=tA + "srl")
                fw.dma("sr", (lambda b=b, jglob=jglob, gi=gi: sy.dma_start(
                    dpair_g(gi), d_gsp[b * N + jglob * 128: b * N + jglob * 128 + 128, :])),
                    reads=[f"hg{b}_{jglob}"], writes=gr_bufs(gi), tag=tA + "grl")
                fw.dma("xt", (lambda b=b, jglob=jglob, xri=xri: sy.dma_start(
                    dpair_x(xri), d_x_tok[b * N + jglob * 128: b * N + jglob * 128 + 128, :])),
                    writes=xr_bufs(xri), tag=tA + "xr2")
                oi = jglob % 2
                for dh in range(2):
                    pb = dh % 2
                    for kt in range(NT):
                        fw.op("tensor", (lambda kt=kt, jt=jt, dh=dh, pb=pb, pp=pp: t.matmul(
                            PB[pb][:], AT_ap(pp, kt, jt * 128, 128),
                            sbf_ap(kt, dh * 512, 512),
                            start=(kt == 0), stop=(kt == NT - 1))),
                            reads=AT_bufs(pp, kt) + sbf_bufs(kt), writes=[psb[pb]],
                            tag=tA + f"pv{qb}")
                    fw.op("vector", (lambda jt=jt, dh=dh, pb=pb, sri=sri: v.scalar_tensor_tensor(
                        MSC[:, 0:512], PB[pb][:], recipd[:, jt:jt + 1],
                        dpair_s(sri)[:, dh * 512:(dh + 1) * 512], ALU.mult, ALU.add)),
                        reads=[psb[pb], "recipd", sr_bufs(sri)[dh]],
                        writes=["D22a", "D22b"], tag=tA + f"cv{qb}")
                    fw.op("vector", (lambda dh=dh: v.tensor_add(
                        MSC[:, 0:512], MSC[:, 0:512], GB[:, dh * 512:(dh + 1) * 512])),
                        reads=["D22a", "D22b"] + [GB_bufs[dh]], writes=["D22a", "D22b"])
                    fw.op("scalar", (lambda dh=dh, oi=oi: sc.activation(
                        dpair_o(oi)[:, dh * 512:(dh + 1) * 512], MSC[:, 0:512], AF.Tanh,
                        scale=betacol[:, 0:1])),
                        reads=["D22a", "D22b", "betacol"], writes=[or_bufs(oi)[dh]],
                        tag=tA + "tanh")
                    fw.op("gpsimd", (lambda dh=dh, oi=oi, gi=gi: gp.tensor_mul(
                        dpair_o(oi)[:, dh * 512:(dh + 1) * 512],
                        dpair_o(oi)[:, dh * 512:(dh + 1) * 512],
                        dpair_g(gi)[:, dh * 512:(dh + 1) * 512])),
                        reads=[or_bufs(oi)[dh], gr_bufs(gi)[dh]], writes=[or_bufs(oi)[dh]])
                    fw.op("gpsimd", (lambda dh=dh, oi=oi, xri=xri: gp.tensor_add(
                        dpair_o(oi)[:, dh * 512:(dh + 1) * 512],
                        dpair_o(oi)[:, dh * 512:(dh + 1) * 512],
                        dpair_x(xri)[:, dh * 512:(dh + 1) * 512])),
                        reads=[or_bufs(oi)[dh], xr_bufs(xri)[dh]], writes=[or_bufs(oi)[dh]])
                fw.dma("gout", (lambda b=b, jglob=jglob, oi=oi: sc.dma_start(
                    d_out[b * N + jglob * 128: b * N + jglob * 128 + 128, :], dpair_o(oi))),
                    reads=or_bufs(oi), writes=[f"ho{b}_{jglob}"], tag=tA + "out")

    for b in range(NB):
        emit_batch(b)

    fw.op("sync", lambda: None,
          reads=[f"ho{b}_{j}" for b in range(NB) for j in range(NT)], tag="final")
    fw.ops[-1]["final"] = True

    with nc.Block() as block:
        fw.emit(block, sems)

    nc._ctx_keepalive = ctx
    return nc


_NC_CACHE = None


def kernel(L0_attn_x, L0_attn_W_spin, L0_attn_b_spin, L0_attn_W_global, L0_attn_W_J,
           L0_attn_beta, L0_ffn_centroids, L0_ffn_expert_biases, L0_ffn_Wr, L0_ffn_br,
           L0_ffn_Ws, L0_ffn_bs):
    global _NC_CACHE
    x = np.asarray(L0_attn_x, dtype=np.float32)

    WrT = np.ascontiguousarray(np.asarray(L0_ffn_Wr, np.float32).transpose(0, 2, 1)).reshape(NE * D, D)
    WsT = np.ascontiguousarray(np.asarray(L0_ffn_Ws, np.float32).transpose(0, 2, 1)).reshape(2 * D, D)
    WspinT = np.ascontiguousarray(np.asarray(L0_attn_W_spin, np.float32).T)
    WJT = np.ascontiguousarray(np.asarray(L0_attn_W_J, np.float32).T)
    WglobT = np.ascontiguousarray(np.asarray(L0_attn_W_global, np.float32).T)
    centT = np.ascontiguousarray(np.asarray(L0_ffn_centroids, np.float32).T)
    br = np.asarray(L0_ffn_br, np.float32)
    bspinB = np.ascontiguousarray(np.broadcast_to(
        np.asarray(L0_attn_b_spin, np.float32)[None, :], (128, D)))
    bs = np.asarray(L0_ffn_bs, np.float32)
    bsB2 = np.ascontiguousarray(np.broadcast_to(bs[:, None, :], (2, 128, D))).reshape(256, D)
    ebB = np.ascontiguousarray(np.broadcast_to(
        np.asarray(L0_ffn_expert_biases, np.float32)[None, :], (128, NE)))
    betacol = np.ascontiguousarray(np.broadcast_to(
        np.asarray(L0_attn_beta, np.float32).reshape(1, 1), (128, 1)))
    identbf = np.eye(128, dtype=ml_dtypes.bfloat16)
    ones_col = np.ones((128, 1), dtype=ml_dtypes.bfloat16)
    ones_row = np.ones((1, 128), dtype=ml_dtypes.bfloat16)
    ones_rowf = np.ones((1, 128), dtype=np.float32)

    if _NC_CACHE is None:
        _NC_CACHE = build_nc()
    nc = _NC_CACHE

    in_maps = []
    for cidx in range(NCORES):
        xs = x[cidx * NB:(cidx + 1) * NB]
        x_tok = np.ascontiguousarray(xs.reshape(NB * N, D))
        x_dm = np.ascontiguousarray(xs.transpose(0, 2, 1).reshape(NB * D, N))
        in_maps.append({
            "x_tok": x_tok, "x_dm": x_dm, "WrT": WrT, "WsT": WsT,
            "WspinT": WspinT, "WJT": WJT, "WglobT": WglobT, "centT": centT,
            "br": br, "bspinB": bspinB, "bsB2": bsB2, "ebB": ebB,
            "betacol": betacol, "identbf": identbf, "ones_col": ones_col,
            "ones_row": ones_row, "ones_rowf": ones_rowf,
        })

    res = run_bass_kernel_spmd(nc, in_maps, core_ids=list(range(NCORES)))
    outs = [np.asarray(r["out"]).reshape(NB, N, D) for r in res.results]
    return np.concatenate(outs, axis=0)
